# version 02
# baseline (speedup 1.0000x reference)
import numpy as np
import ml_dtypes
from contextlib import ExitStack
import concourse.bass as bass
import concourse.mybir as mybir
from concourse.bass import IndirectOffsetOnAxis
from concourse.bass_utils import run_bass_kernel_spmd

F32 = mybir.dt.float32
BF16 = mybir.dt.bfloat16
I32 = mybir.dt.int32
AF = mybir.ActivationFunctionType
ALU = mybir.AluOpType
AX = mybir.AxisListType
BF16NP = ml_dtypes.bfloat16

ENGINES = ("sync", "vector", "scalar", "tensor", "gpsimd")

D = 1024
NB = 32
TPB = 128
TOK = NB * TPB
NCORE = 8
SEQ = 4096
BATCH = 8
NH = 16
HD = 64


class Prog:

    def __init__(self, nc):
        self.nc = nc
        self.ops = {e: [] for e in ENGINES}
        self.cnt = {}
        self.sem_names = []

    def inc_val(self, sem, n):
        self.cnt[sem] = self.cnt.get(sem, 0) + n
        if sem not in self.sem_names:
            self.sem_names.append(sem)
        return self.cnt[sem]

    def now(self, sem):
        return self.cnt.get(sem, 0)

    def after(self, *sems):
        return tuple((s, self.cnt.get(s, 0)) for s in sems)

    AUTO = {"vector": "cv", "scalar": "ca"}

    def emit(self, eng, fn, waits=(), incs=(), chain=True):
        if chain and eng in self.AUTO:
            s = self.AUTO[eng]
            extra = [iv for iv in incs if iv[0] != s]
            assert not extra, f"extra incs {extra} on auto-chained {eng}"
            waits = tuple(waits) + ((s, self.now(s)),)
            incs = ((s, 1),)
        waits = tuple((s, v) for (s, v) in waits if v > 0)
        vals = [self.inc_val(s, n) for (s, n) in incs]
        self.ops[eng].append((fn, waits, tuple((s, n) for (s, n) in incs)))
        return vals[0] if len(vals) == 1 else vals

    def gemit(self, fn, waits=(), incs=()):
        extra = [iv for iv in incs if iv[0] != "cg"]
        assert not extra
        waits = tuple(waits) + (("cg", self.now("cg")),)
        return self.emit("gpsimd", fn, waits, (("cg", 1),), chain=False)

    def dma(self, eng, out, in_, sem="ld", waits=(), **kw):
        return self.emit(eng, lambda E: E.dma_start(out=out, in_=in_, **kw),
                         waits=waits, incs=((sem, 16),), chain=False)

    def build(self, block):
        for eng_name in ENGINES:
            ops = self.ops[eng_name]
            if not ops:
                continue

            def make(ops_local):
                def body(E):
                    for fn, waits, incs in ops_local:
                        for s, v in waits:
                            E.wait_ge(self._sems[s], v)
                        ins = fn(E)
                        for s, n in incs:
                            if ins is None:
                                ins = E.nop()
                            ins = ins.then_inc(self._sems[s], n) if n != 1 else ins.then_inc(self._sems[s])
                return body
            getattr(block, eng_name)(make(list(ops)))


def finish_prog(nc, es, p):
    p._sems = {name: es.enter_context(nc.semaphore(name)) for name in p.sem_names}
    with nc.Block() as block:
        p.build(block)



def core_slice_tokens(x_full):
    Dl = x_full.shape[-1]
    xr = x_full.reshape(BATCH, 32, TPB, Dl)
    return [np.ascontiguousarray(xr[:, c::8].reshape(TOK, Dl)) for c in range(NCORE)]


def core_unslice_tokens(parts):
    Dl = parts[0].shape[-1]
    xr = np.empty((BATCH, 32, TPB, Dl), dtype=parts[0].dtype)
    for c in range(NCORE):
        xr[:, c::8] = parts[c].reshape(BATCH, 4, TPB, Dl)
    return xr.reshape(BATCH, SEQ, Dl)


def blocks_global_order(parts_32rows):
    N = parts_32rows[0].shape[-1]
    out = np.empty((BATCH, 32) + parts_32rows[0].shape[1:][1:], dtype=parts_32rows[0].dtype) \
        if False else np.empty((BATCH, 32, N), dtype=parts_32rows[0].dtype)
    for c in range(NCORE):
        out[:, c::8] = parts_32rows[c].reshape(BATCH, 4, N)
    return out.reshape(256, N)


def host_consts(c):
    u = np.triu(np.ones((128, 128), dtype=np.float32), 1)
    iden = np.eye(128, dtype=np.float32)
    onescol = np.ones((128, 1), dtype=np.float32)
    onesrow = np.ones((1, 512), dtype=np.float32)
    itok = np.empty((128, NB), dtype=np.float32)
    lrow = np.empty((128, NB), dtype=np.float32)
    for k in range(NB):
        jj = k % 4
        itok[:, k] = (c + 8 * jj) * TPB + np.arange(TPB)
        lrow[:, k] = k * TPB + np.arange(TPB)
    eidx = np.broadcast_to(np.arange(8, dtype=np.float32), (128, 8)).copy()
    bdiag = np.zeros((128, 128), dtype=np.float32)
    bdiag[0:64, 0:64] = 1.0; bdiag[64:128, 64:128] = 1.0
    selc = np.zeros((32, 4), dtype=np.float32)
    for jj in range(4):
        selc[: c + 8 * jj, jj] = 1.0
    selg = np.zeros((256, NB), dtype=np.float32)
    for k in range(NB):
        b, jj = k // 4, k % 4
        selg[: b * 32 + c + 8 * jj, k] = 1.0
    return {"c_utri": u, "c_iden": iden, "c_onescol": onescol,
            "c_onesrow": onesrow, "c_itok": itok, "c_lrow": lrow,
            "c_eidx": eidx, "c_selc": selc, "c_selg": selg, "c_bdiag": bdiag}


def decl_consts(nc):
    t = {}
    t["c_utri"] = nc.dram_tensor("c_utri", [128, 128], F32, kind="ExternalInput").ap()
    t["c_iden"] = nc.dram_tensor("c_iden", [128, 128], F32, kind="ExternalInput").ap()
    t["c_onescol"] = nc.dram_tensor("c_onescol", [128, 1], F32, kind="ExternalInput").ap()
    t["c_onesrow"] = nc.dram_tensor("c_onesrow", [1, 512], F32, kind="ExternalInput").ap()
    t["c_itok"] = nc.dram_tensor("c_itok", [128, NB], F32, kind="ExternalInput").ap()
    t["c_lrow"] = nc.dram_tensor("c_lrow", [128, NB], F32, kind="ExternalInput").ap()
    t["c_eidx"] = nc.dram_tensor("c_eidx", [128, 8], F32, kind="ExternalInput").ap()
    t["c_selc"] = nc.dram_tensor("c_selc", [32, 4], F32, kind="ExternalInput").ap()
    t["c_selg"] = nc.dram_tensor("c_selg", [256, NB], F32, kind="ExternalInput").ap()
    t["c_bdiag"] = nc.dram_tensor("c_bdiag", [128, 128], F32, kind="ExternalInput").ap()
    return t


class Consts:

    def __init__(self, nc, es, p, dr):
        sb = lambda name, shape, dt: es.enter_context(nc.sbuf_tensor(name, shape, dt))
        self.utri = sb("k_utri", [128, 128], BF16)
        self.iden = sb("k_iden", [128, 128], BF16)
        self.onescol = sb("k_onescol", [128, 1], BF16)
        self.onesrow = sb("k_onesrow", [1, 512], BF16)
        self.onescol_f = sb("k_onescol_f", [128, 1], F32)
        self.onesrow_f = sb("k_onesrow_f", [1, 512], F32)
        self.itok = sb("k_itok", [128, NB], F32)
        self.lrow = sb("k_lrow", [128, NB], F32)
        self.eidx = sb("k_eidx", [128, 8], F32)
        p.dma("gpsimd", self.utri[:], dr["c_utri"], "wl")
        p.dma("gpsimd", self.iden[:], dr["c_iden"], "wl")
        p.dma("gpsimd", self.onescol[:], dr["c_onescol"], "wl")
        p.dma("gpsimd", self.onesrow[:], dr["c_onesrow"], "wl")
        p.dma("sync", self.onescol_f[:], dr["c_onescol"], "wl")
        p.dma("sync", self.onesrow_f[:], dr["c_onesrow"], "wl")
        p.dma("sync", self.itok[:], dr["c_itok"], "wl")
        p.dma("sync", self.lrow[:], dr["c_lrow"], "wl")
        p.dma("sync", self.eidx[:], dr["c_eidx"], "wl")
        self.ready = p.after("wl")



def emit_ln_stats(p, xb_ap, ssq_scr, mean, rstd, tmp1, eps, wait=()):
    Dl = xb_ap.shape[-1]
    p.emit("vector", lambda E: E.tensor_reduce(out=mean[:].unsqueeze(2),
                                               in_=xb_ap.rearrange("p (a b) -> p a b", a=1),
                                               axis=AX.X, op=ALU.add),
           waits=wait, incs=(("cv", 1),))
    p.emit("scalar", lambda E: E.activation(out=ssq_scr, in_=xb_ap, func=AF.Square,
                                            accum_out=tmp1[:]),
           waits=wait, incs=(("ca", 1),))
    ca = p.now("ca")
    p.emit("vector", lambda E: E.tensor_scalar_mul(out=mean[:], in0=mean[:], scalar1=1.0 / Dl),
           incs=(("cv", 1),))
    p.emit("vector", lambda E: E.tensor_scalar_mul(out=tmp1[:], in0=tmp1[:], scalar1=1.0 / Dl),
           waits=(("ca", ca),), incs=(("cv", 1),))
    p.emit("vector", lambda E: E.tensor_mul(out=rstd[:], in0=mean[:], in1=mean[:]), incs=(("cv", 1),))
    p.emit("vector", lambda E: E.tensor_sub(out=tmp1[:], in0=tmp1[:], in1=rstd[:]), incs=(("cv", 1),))
    p.emit("vector", lambda E: E.tensor_scalar_add(out=tmp1[:], in0=tmp1[:], scalar1=float(eps)),
           incs=(("cv", 1),))
    cv = p.now("cv")
    p.emit("scalar", lambda E: E.activation(out=rstd[:], in_=tmp1[:], func=AF.Sqrt),
           waits=(("cv", cv),), incs=(("ca", 1),))
    ca = p.now("ca")
    p.emit("vector", lambda E: E.reciprocal(out=rstd[:], in_=rstd[:]),
           waits=(("ca", ca),), incs=(("cv", 1),))
    return ("cv", p.now("cv"))


def emit_transpose_set(p, consts, src_ap_fn, dst_tile, nchunks, pT, milestone="cv", wait=(),
                       dst_fn=None):
    if dst_fn is None:
        dst_fn = lambda k: dst_tile[:, k * 128:(k + 1) * 128]
    ev_done = [p.now(milestone)] * 2
    for k in range(nchunks):
        bank = k % 2
        w = list(wait) + [(milestone, ev_done[bank])]
        p.emit("tensor", lambda E, k=k, bank=bank: E.transpose(pT[bank][:], src_ap_fn(k), consts.iden[:]),
               waits=tuple(w), incs=(("ct", 1),))
        ct = p.now("ct")
        p.emit("vector",
               lambda E, k=k, bank=bank: E.tensor_copy(out=dst_fn(k), in_=pT[bank][:]),
               waits=(("ct", ct),), incs=((milestone, 1),))
        ev_done[bank] = p.now(milestone)
    return (milestone, p.now(milestone))




def build_A():
    nc = bass.Bass("TRN2", target_bir_lowering=False, debug=False)
    dr = decl_consts(nc)
    x_in = nc.dram_tensor("x", [TOK, D], F32, kind="ExternalInput").ap()
    wqkv_in = nc.dram_tensor("wqkv", [D, 3 * D], F32, kind="ExternalInput").ap()
    p_out = nc.dram_tensor("p", [TOK, D], BF16, kind="ExternalOutput").ap()
    az_out = nc.dram_tensor("az", [TOK, 32], F32, kind="ExternalOutput").ap()
    bs_out = nc.dram_tensor("bsums", [NB, D], F32, kind="ExternalOutput").ap()
    dbg_qkv = nc.dram_tensor("dbg_qkv", [128, 3 * D], F32, kind="ExternalOutput").ap()
    dbg_s = nc.dram_tensor("dbg_s", [128, 64], F32, kind="ExternalOutput").ap()
    dbg_xn = nc.dram_tensor("dbg_xn", [128, D], F32, kind="ExternalOutput").ap()
    dbg_xnT = nc.dram_tensor("dbg_xnT", [128, D], F32, kind="ExternalOutput").ap()
    dbg_ms = nc.dram_tensor("dbg_ms", [128, 4], F32, kind="ExternalOutput").ap()

    es = ExitStack()
    sb = lambda name, shape, dt=F32: es.enter_context(nc.sbuf_tensor(name, shape, dt))
    ps = lambda name, shape, dt=F32: es.enter_context(nc.psum_tensor(name, shape, dt))
    with es:
        p = Prog(nc)
        consts = Consts(nc, es, p, dr)

        wq = sb("wq", [128, 8 * 3 * D], BF16)
        xb = sb("xb", [128, D])
        scr = sb("scr", [128, D])
        mean = sb("mean", [128, 1]); rstd = sb("rstd", [128, 1]); tmp1 = sb("tmp1", [128, 1])
        xn = sb("xn", [128, D], BF16)
        xnT = sb("xnT", [128, 8 * 128], BF16)
        qkv = sb("qkv", [128, 3 * D])
        vv = sb("vv", [128, D], BF16)
        s16 = sb("s16", [128, 16]); m16 = sb("m16", [128, 16]); es16 = sb("es16", [128, 16])
        a16 = sb("a16", [128, 16]); e16 = sb("e16", [128, 16])
        az = sb("azt", [128, 32])
        pbuf = sb("pbuf", [128, D], BF16)
        bsrow = sb("bsrow", [1, D])
        dbgms = sb("dbgms", [128, 4])

        pT = [ps("pT0", [128, 128], BF16), ps("pT1", [128, 128], BF16)]
        pq = [ps("pq0", [128, 512]), ps("pq1", [128, 512])]
        pS = [ps("pS0", [128, 512]), ps("pS1", [128, 512])]
        pV = ps("pV", [128, 1024])

        p.dma("gpsimd", wq[:].rearrange("p (k n) -> p k n", k=8),
              wqkv_in.rearrange("(k p) n -> p k n", p=128), "wl")
        ld0 = p.now("wl")

        ps_release = 0
        for blk in range(NB):
            xld = p.emit("sync", lambda E, blk=blk: E.dma_start(
                out=xb[:], in_=x_in[blk * TPB:(blk + 1) * TPB, :]),
                waits=p.after("blkf"), incs=(("ld", 16),))
            wx = (("ld", xld),)
            if blk == 0:
                p.emit("vector", lambda E: E.tensor_reduce(out=dbgms[:, 2:3], in_=xb[:],
                                                           axis=AX.X, op=ALU.add),
                       waits=wx, incs=(("cv", 1),))
            st = emit_ln_stats(p, xb[:], scr[:], mean, rstd, tmp1, 1e-5, wait=wx)
            p.emit("vector", lambda E: E.tensor_scalar(out=xn[:], in0=xb[:], scalar1=mean[:],
                                                       scalar2=rstd[:], op0=ALU.subtract, op1=ALU.mult),
                   waits=(st,), incs=(("cv", 1),))
            xn_ready = p.after("cv")
            tr = emit_transpose_set(p, consts, lambda k: xn[:, k * 128:(k + 1) * 128], xnT, 8, pT,
                                    wait=xn_ready + (("wl", ld0),))
            ev = [p.now("cv")] * 2
            for n in range(6):
                bank = n % 2
                for k in range(8):
                    p.emit("tensor", lambda E, n=n, k=k, bank=bank: E.matmul(
                        pq[bank][:], xnT[:, k * 128:(k + 1) * 128],
                        wq[:, k * 3072 + n * 512: k * 3072 + (n + 1) * 512],
                        start=(k == 0), stop=(k == 7)),
                        waits=(tr, ("cv", ev[bank])) if k == 0 else (),
                        incs=((("ct", 1),) if k == 7 else ()))
                ct = p.now("ct")
                p.emit("vector", lambda E, n=n, bank=bank: E.tensor_copy(
                    out=qkv[:, n * 512:(n + 1) * 512], in_=pq[bank][:]),
                    waits=(("ct", ct),), incs=(("cv", 1),))
                ev[bank] = p.now("cv")
            qkv_ready = p.after("cv")
            q3 = qkv[:].rearrange("p (h c) -> p h c", c=192)[:, :, 0:64]
            k3 = qkv[:].rearrange("p (h c) -> p h c", c=192)[:, :, 64:128]
            v3 = qkv[:].rearrange("p (h c) -> p h c", c=192)[:, :, 128:192]
            scr3 = scr[:].rearrange("p (h c) -> p h c", c=64)
            p.emit("vector", lambda E: E.tensor_tensor(out=scr3, in0=q3, in1=k3, op=ALU.mult),
                   waits=qkv_ready, incs=(("cv", 1),))
            p.emit("vector", lambda E: E.tensor_reduce(out=s16[:].unsqueeze(2), in_=scr3,
                                                       axis=AX.X, op=ALU.add), incs=(("cv", 1),))
            p.emit("vector", lambda E: E.tensor_scalar_mul(out=s16[:], in0=s16[:], scalar1=0.125),
                   incs=(("cv", 1),))
            p.emit("vector", lambda E: E.tensor_scalar_max(out=m16[:], in0=s16[:], scalar1=0.0),
                   incs=(("cv", 1),))
            p.emit("vector", lambda E: E.tensor_sub(out=es16[:], in0=s16[:], in1=m16[:]),
                   incs=(("cv", 1),))
            cv = p.now("cv")
            p.emit("scalar", lambda E: E.activation(out=a16[:], in_=m16[:], func=AF.Exp, scale=-1.0),
                   waits=(("cv", cv),), incs=(("ca", 1),))
            p.emit("scalar", lambda E: E.activation(out=e16[:], in_=es16[:], func=AF.Exp),
                   incs=(("ca", 1),))
            exp_ready = p.after("ca")
            p.emit("vector", lambda E: E.tensor_copy(out=vv[:].rearrange("p (h c) -> p h c", c=64), in_=v3),
                   waits=qkv_ready, incs=(("cv", 1),))
            vv_ready = p.after("cv")
            for n2 in range(2):
                p.emit("tensor", lambda E, n2=n2: E.matmul(
                    pS[n2][:], consts.utri[:], vv[:, n2 * 512:(n2 + 1) * 512], start=True, stop=True),
                    waits=vv_ready + consts.ready + (("cv", ps_release),), incs=(("ct", 1),))
                p.emit("tensor", lambda E, n2=n2: E.matmul(
                    pV[0:1, n2 * 512:(n2 + 1) * 512], consts.onescol[:], vv[:, n2 * 512:(n2 + 1) * 512],
                    start=True, stop=True),
                    incs=(("ct", 1),))
            ct = p.now("ct")
            views = []
            for n2 in range(2):
                views.append((
                    pS[n2][:].rearrange("p (h c) -> p h c", c=64),
                    a16[:, n2 * 8:(n2 + 1) * 8].unsqueeze(2).broadcast_to([128, 8, 64]),
                    e16[:, n2 * 8:(n2 + 1) * 8].unsqueeze(2).broadcast_to([128, 8, 64]),
                    vv[:, n2 * 512:(n2 + 1) * 512].rearrange("p (h c) -> p h c", c=64),
                    scr[:, n2 * 512:(n2 + 1) * 512].rearrange("p (h c) -> p h c", c=64),
                    pbuf[:, n2 * 512:(n2 + 1) * 512].rearrange("p (h c) -> p h c", c=64),
                ))
            for n2 in range(2):
                Sx3, a3, e3, v3h, scr3h, pb3 = views[n2]
                p.emit("vector", lambda E, Sx3=Sx3, a3=a3, scr3h=scr3h: E.tensor_tensor(
                    out=scr3h, in0=Sx3, in1=a3, op=ALU.mult),
                    waits=(("ct", ct),) + exp_ready, incs=(("cv", 1),))
            ps_release = p.now("cv")
            if True:
                pass
            for n2 in range(2):
                Sx3, a3, e3, v3h, scr3h, pb3 = views[n2]
                p.emit("vector", lambda E, v3h=v3h, e3=e3, pb3=pb3: E.tensor_tensor(
                    out=pb3, in0=v3h, in1=e3, op=ALU.mult), incs=(("cv", 1),))
                p.emit("vector", lambda E, pb3=pb3, scr3h=scr3h: E.tensor_tensor(
                    out=pb3, in0=pb3, in1=scr3h, op=ALU.add), incs=(("cv", 1),))
            p.emit("vector", lambda E: E.tensor_copy(out=bsrow[:], in_=pV[0:1, :]),
                   waits=(("ct", ct),), incs=(("cv", 1),))
            cv = p.now("cv")
            p.emit("sync", lambda E, blk=blk: E.dma_start(out=bs_out[blk:blk + 1, :], in_=bsrow[:]),
                   waits=(("cv", cv),), incs=(("st", 16),))
            p.emit("vector", lambda E, blk=blk: E.tensor_scalar(
                out=az[:, 0:16], in0=a16[:], scalar1=consts.itok[:, blk:blk + 1], op0=ALU.mult,
                scalar2=None), waits=exp_ready + consts.ready, incs=(("cv", 1),))
            p.emit("vector", lambda E: E.tensor_add(out=az[:, 0:16], in0=az[:, 0:16], in1=e16[:]),
                   incs=(("cv", 1),))
            p.emit("vector", lambda E: E.reciprocal(out=az[:, 16:32], in_=az[:, 0:16]),
                   incs=(("cv", 1),))
            p.emit("vector", lambda E: E.tensor_copy(out=az[:, 0:16], in_=a16[:]),
                   incs=(("cv", 1),))
            cv = p.now("cv")
            if blk == 0:
                p.emit("vector", lambda E: E.tensor_copy(out=dbgms[:, 0:1], in_=mean[:]),
                       waits=(("cv", cv),), incs=(("cv", 1),))
                p.emit("vector", lambda E: E.tensor_copy(out=dbgms[:, 1:2], in_=rstd[:]),
                       incs=(("cv", 1),))
                cvd = p.now("cv")
                p.emit("sync", lambda E: E.dma_start(out=dbg_ms, in_=dbgms[:]),
                       waits=(("cv", cvd),), incs=(("st", 16),))
                p.emit("gpsimd", lambda E: E.dma_start(out=dbg_xn, in_=xn[:]),
                       waits=(("cv", cv),), incs=(("st", 16),))
                p.emit("gpsimd", lambda E: E.dma_start(out=dbg_xnT, in_=xnT[:]),
                       waits=(("cv", cv),), incs=(("st", 16),))
                p.emit("sync", lambda E: E.dma_start(out=dbg_qkv, in_=qkv[:]),
                       waits=(("cv", cv),), incs=(("st", 16),))
                p.emit("sync", lambda E: E.dma_start(out=dbg_s[:, 0:16], in_=s16[:]),
                       waits=(("cv", cv),), incs=(("st", 16),))
                p.emit("sync", lambda E: E.dma_start(out=dbg_s[:, 16:32], in_=m16[:]),
                       waits=(("cv", cv),), incs=(("st", 16),))
                p.emit("sync", lambda E: E.dma_start(out=dbg_s[:, 32:48], in_=a16[:]),
                       waits=(("cv", cv),), incs=(("st", 16),))
                p.emit("sync", lambda E: E.dma_start(out=dbg_s[:, 48:64], in_=e16[:]),
                       waits=(("cv", cv),), incs=(("st", 16),))
            p.emit("sync", lambda E, blk=blk: E.dma_start(out=az_out[blk * TPB:(blk + 1) * TPB, :], in_=az[:]),
                   waits=(("cv", cv),), incs=(("st", 16),))
            p.emit("sync", lambda E, blk=blk: E.dma_start(out=p_out[blk * TPB:(blk + 1) * TPB, :], in_=pbuf[:]),
                   waits=(("cv", cv),), incs=(("st", 16),))
            st = p.now("st")
            p.emit("sync", lambda E: E.nop(), waits=(("st", st),), incs=(("blkf", 1),))
        p.emit("sync", lambda E: E.nop(), waits=(("st", p.now("st")),))
        finish_prog(nc, es, p)
    return nc


def run_A(inputs, xs_parts):
    nc = build_A()
    in_maps = []
    for c in range(NCORE):
        m = dict(host_consts(c))
        m["x"] = xs_parts[c]
        m["wqkv"] = inputs["L0_attn_wqkv"]
        in_maps.append(m)
    res = run_bass_kernel_spmd(nc, in_maps, core_ids=list(range(NCORE)))
    return res.results




def build_B():
    nc = bass.Bass("TRN2", target_bir_lowering=False, debug=False)
    dr = decl_consts(nc)
    x_in = nc.dram_tensor("x", [TOK, D], F32, kind="ExternalInput").ap()
    p_in = nc.dram_tensor("p", [TOK, D], BF16, kind="ExternalInput").ap()
    az_in = nc.dram_tensor("az", [TOK, 32], F32, kind="ExternalInput").ap()
    bs_in = nc.dram_tensor("bs_all", [BATCH, 32, D], F32, kind="ExternalInput").ap()
    rw_in = nc.dram_tensor("router_w", [8, D], F32, kind="ExternalInput").ap()
    x1_out = nc.dram_tensor("x1", [TOK, D], F32, kind="ExternalOutput").ap()
    xf_out = nc.dram_tensor("xf", [TOK, D], BF16, kind="ExternalOutput").ap()
    rt_out = nc.dram_tensor("route", [TOK, 4], F32, kind="ExternalOutput").ap()
    cnt_out = nc.dram_tensor("counts", [NB, 8], F32, kind="ExternalOutput").ap()

    es = ExitStack()
    sb = lambda name, shape, dt=F32: es.enter_context(nc.sbuf_tensor(name, shape, dt))
    ps = lambda name, shape, dt=F32: es.enter_context(nc.psum_tensor(name, shape, dt))
    with es:
        p = Prog(nc)
        consts = Consts(nc, es, p, dr)

        xb = sb("xb", [128, D])
        pb = sb("pb", [128, D], BF16)
        azt = sb("azt", [128, 32])
        scr = sb("scr", [128, D])
        x1t = sb("x1t", [128, D])
        xf = sb("xft", [128, D], BF16)
        xfT = sb("xfT", [128, 8 * 128], BF16)
        mean = sb("mean", [128, 1]); rstd = sb("rstd", [128, 1]); tmp1 = sb("tmp1", [128, 1])
        bst = sb("bst", [32, D], BF16)
        selcb = sb("selcb", [32, 4], BF16)
        rw8 = sb("rw8", [8, D], BF16)
        rwT = sb("rwT", [128, 64], BF16)
        offrow = sb("offrow", [1, D])
        offrowb = sb("offrowb", [1, D], BF16)
        rr = sb("rr", [128, 1])
        lg = sb("lg", [128, 8]); ex8 = sb("ex8", [128, 8]); pr8 = sb("pr8", [128, 8])
        eq8 = sb("eq8", [128, 8]); cd8 = sb("cd8", [128, 8]); oh1 = sb("oh1", [128, 8])
        oh2 = sb("oh2", [128, 8]); pr2 = sb("pr2", [128, 8]); m8b = sb("m8b", [128, 8], BF16)
        mx1 = sb("mx1", [128, 1]); mx2 = sb("mx2", [128, 1]); i1 = sb("i1", [128, 1])
        i2 = sb("i2", [128, 1]); wsum = sb("wsum", [128, 1])
        rt = sb("rt", [128, 4])
        cntrow = sb("cntrow", [1, 8])

        pT = [ps("pT0", [128, 128], BF16), ps("pT1", [128, 128], BF16)]
        pOff = ps("pOff", [128, 1024])
        pOffB = ps("pOffB", [128, 1024])
        pL = ps("pL", [128, 8])

        p.dma("gpsimd", selcb[:], dr["c_selc"], "wl")
        p.dma("gpsimd", rw8[:], rw_in, "wl")
        wl0 = p.now("wl")
        for k in range(8):
            bank = k % 2
            p.emit("tensor", lambda E, k=k, bank=bank: E.transpose(
                pT[bank][:, 0:8], rw8[0:8, k * 128:(k + 1) * 128], consts.iden[0:8, 0:8]),
                waits=(("wl", wl0),) + consts.ready + (("cv", p.now("cv")),), incs=(("ct", 1),))
            ct = p.now("ct")
            p.emit("vector", lambda E, k=k, bank=bank: E.tensor_copy(
                out=rwT[:, k * 8:(k + 1) * 8], in_=pT[bank][:, 0:8]),
                waits=(("ct", ct),), incs=(("cv", 1),))
        rwT_ready = p.after("cv")

        for blk in range(NB):
            b, jj = blk // 4, blk % 4
            gate = p.after("blkf")
            if jj == 0:
                p.emit("gpsimd", lambda E, b=b: E.dma_start(out=bst[:], in_=bs_in[b]),
                       waits=gate, incs=(("wl", 16),), chain=False)
            bstv = p.now("wl")
            xld = p.emit("sync", lambda E, blk=blk: E.dma_start(
                out=xb[:], in_=x_in[blk * TPB:(blk + 1) * TPB, :]), waits=gate, incs=(("ld", 16),))
            p.emit("sync", lambda E, blk=blk: E.dma_start(
                out=pb[:], in_=p_in[blk * TPB:(blk + 1) * TPB, :]), incs=(("ld", 16),))
            p.emit("sync", lambda E, blk=blk: E.dma_start(
                out=azt[:], in_=az_in[blk * TPB:(blk + 1) * TPB, :]), incs=(("ld", 16),))
            ld = p.now("ld")
            for n2 in range(2):
                p.emit("tensor", lambda E, jj=jj, n2=n2: E.matmul(
                    pOff[0:1, n2 * 512:(n2 + 1) * 512], selcb[:, jj:jj + 1],
                    bst[:, n2 * 512:(n2 + 1) * 512], start=True, stop=True),
                    waits=(("wl", bstv), ("cv", p.now("cv"))), incs=(("ct", 1),))
            ct = p.now("ct")
            p.emit("vector", lambda E: E.tensor_copy(out=offrow[:], in_=pOff[0:1, :]),
                   waits=(("ct", ct),), incs=(("cv", 1),))
            p.emit("vector", lambda E: E.tensor_copy(out=offrowb[:], in_=offrow[:]),
                   incs=(("cv", 1),))
            cv = p.now("cv")
            for n2 in range(2):
                p.emit("tensor", lambda E, n2=n2: E.matmul(
                    pOffB[:, n2 * 512:(n2 + 1) * 512], consts.onesrow[0:1, 0:128],
                    offrowb[0:1, n2 * 512:(n2 + 1) * 512], start=True, stop=True),
                    waits=(("cv", cv),), incs=(("ct", 1),))
            ct = p.now("ct")
            scr3 = scr[:].rearrange("p (h c) -> p h c", c=64)
            p3 = pb[:].rearrange("p (h c) -> p h c", c=64)
            off3 = pOffB[:].rearrange("p (h c) -> p h c", c=64)
            a3 = azt[:, 0:16].unsqueeze(2).broadcast_to([128, 16, 64])
            iz3 = azt[:, 16:32].unsqueeze(2).broadcast_to([128, 16, 64])
            p.emit("vector", lambda E: E.tensor_tensor(out=scr3, in0=off3, in1=a3, op=ALU.mult),
                   waits=(("ct", ct), ("ld", ld)), incs=(("cv", 1),))
            p.emit("vector", lambda E: E.tensor_tensor(out=scr3, in0=scr3, in1=p3, op=ALU.add),
                   incs=(("cv", 1),))
            p.emit("vector", lambda E: E.tensor_tensor(out=scr3, in0=scr3, in1=iz3, op=ALU.mult),
                   incs=(("cv", 1),))
            p.emit("vector", lambda E: E.tensor_add(out=x1t[:], in0=xb[:], in1=scr[:]),
                   incs=(("cv", 1),))
            cv = p.now("cv")
            p.emit("sync", lambda E, blk=blk: E.dma_start(
                out=x1_out[blk * TPB:(blk + 1) * TPB, :], in_=x1t[:]),
                waits=(("cv", cv),), incs=(("st", 16),))
            st = emit_ln_stats(p, x1t[:], scr[:], mean, rstd, tmp1, 1e-5, wait=p.after("cv"))
            p.emit("vector", lambda E: E.tensor_scalar(out=xf[:], in0=x1t[:], scalar1=mean[:],
                                                       scalar2=rstd[:], op0=ALU.subtract, op1=ALU.mult),
                   waits=(st,), incs=(("cv", 1),))
            cv = p.now("cv")
            p.emit("sync", lambda E, blk=blk: E.dma_start(
                out=xf_out[blk * TPB:(blk + 1) * TPB, :], in_=xf[:]),
                waits=(("cv", cv),), incs=(("st", 16),))
            tr = emit_transpose_set(p, consts, lambda k: xf[:, k * 128:(k + 1) * 128], xfT, 8, pT,
                                    wait=p.after("cv"))
            p.emit("scalar", lambda E: E.activation(out=scr[:], in_=xf[:], func=AF.Square,
                                                    accum_out=tmp1[:]),
                   waits=(("cv", cv),), incs=(("ca", 1),))
            ca = p.now("ca")
            p.emit("vector", lambda E: E.tensor_scalar(out=rr[:], in0=tmp1[:], scalar1=1.0 / D,
                                                       scalar2=float(np.finfo(np.float32).eps),
                                                       op0=ALU.mult, op1=ALU.add),
                   waits=(("ca", ca),), incs=(("cv", 1),))
            cv = p.now("cv")
            p.emit("scalar", lambda E: E.activation(out=rr[:], in_=rr[:], func=AF.Sqrt),
                   waits=(("cv", cv),), incs=(("ca", 1),))
            ca = p.now("ca")
            p.emit("vector", lambda E: E.reciprocal(out=rr[:], in_=rr[:]),
                   waits=(("ca", ca),), incs=(("cv", 1),))
            for k in range(8):
                p.emit("tensor", lambda E, k=k: E.matmul(
                    pL[:], xfT[:, k * 128:(k + 1) * 128], rwT[:, k * 8:(k + 1) * 8],
                    start=(k == 0), stop=(k == 7)),
                    waits=(tr,) + rwT_ready if k == 0 else (), incs=((("ct", 1),) if k == 7 else ()))
            ct = p.now("ct")
            p.emit("vector", lambda E: E.tensor_scalar(out=lg[:], in0=pL[:], scalar1=rr[:],
                                                       scalar2=None, op0=ALU.mult),
                   waits=(("ct", ct),), incs=(("cv", 1),))
            p.emit("vector", lambda E: E.tensor_reduce(out=mx1[:], in_=lg[:], axis=AX.X, op=ALU.max),
                   incs=(("cv", 1),))
            p.emit("vector", lambda E: E.tensor_scalar(out=ex8[:], in0=lg[:], scalar1=mx1[:],
                                                       scalar2=None, op0=ALU.subtract),
                   incs=(("cv", 1),))
            cv = p.now("cv")
            p.emit("scalar", lambda E: E.activation(out=pr8[:], in_=ex8[:], func=AF.Exp),
                   waits=(("cv", cv),), incs=(("ca", 1),))
            ca = p.now("ca")
            p.emit("vector", lambda E: E.tensor_reduce(out=wsum[:], in_=pr8[:], axis=AX.X, op=ALU.add),
                   waits=(("ca", ca),), incs=(("cv", 1),))
            p.emit("vector", lambda E: E.reciprocal(out=wsum[:], in_=wsum[:]), incs=(("cv", 1),))
            p.emit("vector", lambda E: E.tensor_scalar(out=pr8[:], in0=pr8[:], scalar1=wsum[:],
                                                       scalar2=None, op0=ALU.mult),
                   incs=(("cv", 1),))
            p.emit("vector", lambda E: E.tensor_reduce(out=mx1[:], in_=pr8[:], axis=AX.X, op=ALU.max),
                   incs=(("cv", 1),))
            p.emit("vector", lambda E: E.tensor_scalar(out=eq8[:], in0=pr8[:], scalar1=mx1[:],
                                                       scalar2=None, op0=ALU.is_ge),
                   incs=(("cv", 1),))
            p.emit("vector", lambda E: E.tensor_scalar(out=cd8[:], in0=eq8[:], scalar1=-1e9,
                                                       scalar2=1e9, op0=ALU.mult, op1=ALU.add),
                   incs=(("cv", 1),))
            p.emit("vector", lambda E: E.tensor_add(out=cd8[:], in0=cd8[:], in1=consts.eidx[:]),
                   waits=consts.ready, incs=(("cv", 1),))
            p.emit("vector", lambda E: E.tensor_reduce(out=i1[:], in_=cd8[:], axis=AX.X, op=ALU.min),
                   incs=(("cv", 1),))
            p.emit("vector", lambda E: E.tensor_scalar(out=oh1[:], in0=consts.eidx[:], scalar1=i1[:],
                                                       scalar2=None, op0=ALU.is_equal),
                   incs=(("cv", 1),))
            p.emit("vector", lambda E: E.tensor_scalar(out=pr2[:], in0=oh1[:], scalar1=-1.0,
                                                       scalar2=1.0, op0=ALU.mult, op1=ALU.add),
                   incs=(("cv", 1),))
            p.emit("vector", lambda E: E.tensor_mul(out=pr2[:], in0=pr2[:], in1=pr8[:]),
                   incs=(("cv", 1),))
            p.emit("vector", lambda E: E.tensor_reduce(out=mx2[:], in_=pr2[:], axis=AX.X, op=ALU.max),
                   incs=(("cv", 1),))
            p.emit("vector", lambda E: E.tensor_scalar(out=eq8[:], in0=pr2[:], scalar1=mx2[:],
                                                       scalar2=None, op0=ALU.is_ge),
                   incs=(("cv", 1),))
            p.emit("vector", lambda E: E.tensor_scalar(out=cd8[:], in0=eq8[:], scalar1=-1e9,
                                                       scalar2=1e9, op0=ALU.mult, op1=ALU.add),
                   incs=(("cv", 1),))
            p.emit("vector", lambda E: E.tensor_add(out=cd8[:], in0=cd8[:], in1=consts.eidx[:]),
                   incs=(("cv", 1),))
            p.emit("vector", lambda E: E.tensor_reduce(out=i2[:], in_=cd8[:], axis=AX.X, op=ALU.min),
                   incs=(("cv", 1),))
            p.emit("vector", lambda E: E.tensor_scalar(out=oh2[:], in0=consts.eidx[:], scalar1=i2[:],
                                                       scalar2=None, op0=ALU.is_equal),
                   incs=(("cv", 1),))
            p.emit("vector", lambda E: E.tensor_add(out=wsum[:], in0=mx1[:], in1=mx2[:]),
                   incs=(("cv", 1),))
            p.emit("vector", lambda E: E.reciprocal(out=wsum[:], in_=wsum[:]), incs=(("cv", 1),))
            p.emit("vector", lambda E: E.tensor_copy(out=rt[:, 0:1], in_=i1[:]), incs=(("cv", 1),))
            p.emit("vector", lambda E: E.tensor_copy(out=rt[:, 1:2], in_=i2[:]), incs=(("cv", 1),))
            p.emit("vector", lambda E: E.tensor_mul(out=rt[:, 2:3], in0=mx1[:], in1=wsum[:]),
                   incs=(("cv", 1),))
            p.emit("vector", lambda E: E.tensor_mul(out=rt[:, 3:4], in0=mx2[:], in1=wsum[:]),
                   incs=(("cv", 1),))
            p.emit("vector", lambda E: E.tensor_add(out=cd8[:], in0=oh1[:], in1=oh2[:]),
                   incs=(("cv", 1),))
            p.emit("vector", lambda E: E.tensor_copy(out=m8b[:], in_=cd8[:]), incs=(("cv", 1),))
            cv = p.now("cv")
            p.emit("tensor", lambda E: E.matmul(pL[0:1, 0:8], consts.onescol[:], m8b[:],
                                                start=True, stop=True),
                   waits=(("cv", cv),), incs=(("ct", 1),))
            ct = p.now("ct")
            p.emit("vector", lambda E: E.tensor_copy(out=cntrow[:], in_=pL[0:1, 0:8]),
                   waits=(("ct", ct),), incs=(("cv", 1),))
            cv = p.now("cv")
            p.emit("sync", lambda E, blk=blk: E.dma_start(out=cnt_out[blk:blk + 1, :], in_=cntrow[:]),
                   waits=(("cv", cv),), incs=(("st", 16),))
            p.emit("sync", lambda E, blk=blk: E.dma_start(
                out=rt_out[blk * TPB:(blk + 1) * TPB, :], in_=rt[:]),
                waits=(("cv", cv),), incs=(("st", 16),))
            st = p.now("st")
            p.emit("sync", lambda E: E.nop(), waits=(("st", st),), incs=(("blkf", 1),))
        p.emit("sync", lambda E: E.nop(), waits=(("st", p.now("st")),))
        finish_prog(nc, es, p)
    return nc


def run_B(inputs, xs_parts, resA):
    nc = build_B()
    bs_all = blocks_global_order([np.asarray(resA[c]["bsums"]) for c in range(NCORE)]).reshape(BATCH, 32, D)
    in_maps = []
    for c in range(NCORE):
        m = dict(host_consts(c))
        m["x"] = xs_parts[c]
        m["p"] = np.asarray(resA[c]["p"])
        m["az"] = np.asarray(resA[c]["az"])
        m["bs_all"] = bs_all
        m["router_w"] = inputs["L0_ffn_router_w"]
        in_maps.append(m)
    res = run_bass_kernel_spmd(nc, in_maps, core_ids=list(range(NCORE)))
    return res.results



C0 = 1024
SLOTS_C = 8 * C0
OOBROW = 1 << 20


def build_C():
    nc = bass.Bass("TRN2", target_bir_lowering=False, debug=False)
    dr = decl_consts(nc)
    x1_in = nc.dram_tensor("x1", [TOK, D], F32, kind="ExternalInput").ap()
    xf_in = nc.dram_tensor("xf", [TOK, D], BF16, kind="ExternalInput").ap()
    rt_in = nc.dram_tensor("route", [TOK, 4], F32, kind="ExternalInput").ap()
    ca_in = nc.dram_tensor("counts_all", [256, 8], F32, kind="ExternalInput").ap()
    rg_in = nc.dram_tensor("rg", [8, D, 4 * D], F32, kind="ExternalInput").ap()
    ru_in = nc.dram_tensor("ru", [8, D, 4 * D], F32, kind="ExternalInput").ap()
    rd_in = nc.dram_tensor("rd", [8, 4 * D, D], F32, kind="ExternalInput").ap()
    sg_in = nc.dram_tensor("sg", [D, 8 * D], F32, kind="ExternalInput").ap()
    su_in = nc.dram_tensor("su", [D, 8 * D], F32, kind="ExternalInput").ap()
    sd_in = nc.dram_tensor("sd", [8 * D, D], F32, kind="ExternalInput").ap()
    x2_out = nc.dram_tensor("x2", [TOK, D], F32, kind="ExternalOutput").ap()

    rec_dram = nc.dram_tensor("rec", [SLOTS_C, 3], I32, kind="ExternalOutput").ap()
    a_dram = nc.dram_tensor("abuf", [2 * TOK, D], BF16, kind="ExternalOutput").ap()
    offg_dram = nc.dram_tensor("offg", [NB, 8], F32, kind="ExternalOutput").ap()

    es = ExitStack()
    sb = lambda name, shape, dt=F32: es.enter_context(nc.sbuf_tensor(name, shape, dt))
    ps = lambda name, shape, dt=F32: es.enter_context(nc.psum_tensor(name, shape, dt))
    with es:
        p = Prog(nc)
        consts = Consts(nc, es, p, dr)

        gbuf = sb("gbuf", [128, 32 * 1024], BF16)
        xgT = sb("xgT", [128, 8 * 1024], BF16)
        ybuf = sb("ybuf", [128, 8 * 1024], BF16)
        wch = sb("wch", [128, 8 * 128], BF16)
        wch2 = sb("wch2", [128, 8 * 128], BF16)
        wchd = sb("wchd", [128, 512], BF16)
        sil = sb("sil", [128, 512])
        rt = sb("rtc", [128, 4])
        oh1 = sb("oh1", [128, 8]); oh2 = sb("oh2", [128, 8]); m8 = sb("m8", [128, 8])
        m8b = sb("m8bc", [128, 8], BF16); mvb = sb("mvb", [128, 8], BF16)
        sp8 = sb("sp8", [128, 8]); gpre = sb("gpre", [128, 8]); mv = sb("mv", [128, 8])
        t8 = sb("t8", [128, 8]); slot8 = sb("slot8", [128, 8]); mv2 = sb("mv2", [128, 8])
        mv2b = sb("mv2b", [128, 8], BF16)
        eC0 = sb("eC0", [128, 8])
        locoff = sb("locoff", [1, 8])
        cntrow = sb("cntrow", [1, 8])
        offgrow = sb("offgrow", [1, 8])
        sA = sb("sA", [128, 1]); vA = sb("vA", [128, 1]); idxf = sb("idxf", [128, 1])
        idxi = sb("idxi", [128, 1], I32)
        svals = sb("svals", [128, 3], I32)
        cnt2 = [sb("cnt2a", [128, 8], BF16), sb("cnt2b", [128, 8], BF16)]
        selg2 = [sb("selg2a", [128, NB], BF16), sb("selg2b", [128, NB], BF16)]
        recs = sb("recs", [128, 3], I32)
        wslot = sb("wslot", [128, 8])
        adsti = sb("adsti", [128, 8], I32)
        xsT = sb("xsT", [128, 8 * 512], BF16)
        xbt = sb("xbt", [128, D], BF16)
        x1t = sb("x1t", [128, D])
        ysh = sb("ysh", [128, D])
        a1t = sb("a1t", [128, D], BF16); a2t = sb("a2t", [128, D], BF16)
        zt = sb("zt", [128, D], BF16)
        prefill = sb("prefill", [128, 64 * 3], I32)

        pT = [ps("pT0", [128, 128], BF16), ps("pT1", [128, 128], BF16)]
        pH = [ps("pH0", [128, 512]), ps("pH1", [128, 512])]
        pY2 = [ps("pY0", [128, 512]), ps("pY1", [128, 512])]
        pY = [pH[0], pH[1], pY2[0], pY2[1]]
        p8a = ps("p8a", [128, 8])
        p8b = ps("p8b", [128, 8])

        bcreg = {}
        def _init_regs(E):
            r1 = E.alloc_register("bc_rec"); bcreg["rec"] = r1
            r2 = E.alloc_register("bc_a"); bcreg["a"] = r2
            E.reg_mov(r1, SLOTS_C - 1)
            return E.reg_mov(r2, 2 * TOK - 1)
        p.emit("gpsimd", _init_regs)
        pv = prefill[:].rearrange("p (t c) -> p t c", c=3)
        p.emit("vector", lambda E: E.memset(pv[:, :, 0:1], 0), incs=(("cv", 1),))
        p.emit("vector", lambda E: E.memset(pv[:, :, 1:2], OOBROW), incs=(("cv", 1),))
        p.emit("vector", lambda E: E.memset(pv[:, :, 2:3], 0), incs=(("cv", 1),))
        p.emit("vector", lambda E: E.memset(zt[:], 0.0), incs=(("cv", 1),))
        p.emit("vector", lambda E: E.memset(locoff[:], 0.0), incs=(("cv", 1),))
        p.emit("vector", lambda E: E.tensor_scalar_mul(out=eC0[:], in0=consts.eidx[:], scalar1=float(C0)),
               waits=consts.ready, incs=(("cv", 1),))
        cv0 = p.now("cv")
        p.emit("sync", lambda E: E.dma_start(out=rec_dram.rearrange("(t p) c -> p t c", p=128),
                                             in_=pv),
               waits=(("cv", cv0),), incs=(("st", 16),))
        for z in range(2 * NB):
            p.emit("sync", lambda E, z=z: E.dma_start(out=a_dram[z * 128:(z + 1) * 128, :], in_=zt[:]),
                   waits=(("cv", cv0),), incs=(("st", 16),))
        p.dma("gpsimd", cnt2[0][:], ca_in[0:128, :], "wl")
        p.dma("gpsimd", cnt2[1][:], ca_in[128:256, :], "wl")
        p.dma("gpsimd", selg2[0][:], dr["c_selg"][0:128, :], "wl")
        p.dma("gpsimd", selg2[1][:], dr["c_selg"][128:256, :], "wl")
        wl = p.now("wl")
        for h in range(2):
            p.emit("tensor", lambda E, h=h: E.matmul(p8a[0:NB, :], selg2[h][:], cnt2[h][:],
                                                     start=(h == 0), stop=(h == 1)),
                   waits=(("wl", wl),), incs=(("ct", 1),))
        ct = p.now("ct")
        p.emit("vector", lambda E: E.tensor_copy(out=wslot[0:NB, 0:8], in_=p8a[0:NB, :]),
               waits=(("ct", ct),), incs=(("cv", 1),))
        cv = p.now("cv")
        p.emit("sync", lambda E: E.dma_start(out=offg_dram, in_=wslot[0:NB, 0:8]),
               waits=(("cv", cv),), incs=(("st", 16),))
        st_offg = p.now("st")

        for blk in range(NB):
            ld = p.dma("sync", rt[:], rt_in[blk * TPB:(blk + 1) * TPB, :], "ld",
                       waits=p.after("cv"))
            ld = p.dma("sync", offgrow[:], offg_dram[blk:blk + 1, :], "ld",
                       waits=(("st", st_offg),))
            p.emit("tensor", lambda E: E.matmul(p8a[:], consts.onesrow_f[0:1, 0:128], offgrow[:],
                                                start=True, stop=True),
                   waits=(("ld", ld), ("cv", p.now("cv"))) + consts.ready, incs=(("ct", 1),))
            p.emit("vector", lambda E: E.tensor_scalar(out=oh1[:], in0=consts.eidx[:],
                                                       scalar1=rt[:, 0:1], scalar2=None, op0=ALU.is_equal),
                   waits=(("ld", ld),), incs=(("cv", 1),))
            p.emit("vector", lambda E: E.tensor_scalar(out=oh2[:], in0=consts.eidx[:],
                                                       scalar1=rt[:, 1:2], scalar2=None, op0=ALU.is_equal),
                   incs=(("cv", 1),))
            p.emit("vector", lambda E: E.tensor_add(out=m8[:], in0=oh1[:], in1=oh2[:]), incs=(("cv", 1),))
            p.emit("vector", lambda E: E.tensor_copy(out=m8b[:], in_=m8[:]), incs=(("cv", 1),))
            cv = p.now("cv")
            p.emit("tensor", lambda E: E.matmul(p8b[:], consts.utri[:], m8b[:], start=True, stop=True),
                   waits=(("cv", cv),), incs=(("ct", 1),))
            ct = p.now("ct")
            p.emit("vector", lambda E: E.tensor_copy(out=sp8[:], in_=p8b[:]),
                   waits=(("ct", ct),), incs=(("cv", 1),))
            p.emit("vector", lambda E: E.tensor_tensor(out=gpre[:], in0=p8a[:], in1=sp8[:], op=ALU.add),
                   incs=(("cv", 1),))
            p.emit("vector", lambda E: E.tensor_scalar(out=mv[:], in0=gpre[:], scalar1=5120.0,
                                                       scalar2=None, op0=ALU.is_lt), incs=(("cv", 1),))
            p.emit("vector", lambda E: E.tensor_mul(out=mv[:], in0=mv[:], in1=m8[:]), incs=(("cv", 1),))
            p.emit("vector", lambda E: E.tensor_copy(out=mvb[:], in_=mv[:]), incs=(("cv", 1),))
            cv = p.now("cv")
            p.emit("tensor", lambda E: E.matmul(p8a[:], consts.onesrow_f[0:1, 0:128], locoff[:],
                                                start=True, stop=True),
                   waits=(("cv", cv),), incs=(("ct", 1),))
            p.emit("tensor", lambda E: E.matmul(p8b[:], consts.utri[:], mvb[:], start=True, stop=True),
                   incs=(("ct", 1),))
            ct = p.now("ct")
            p.emit("vector", lambda E: E.tensor_copy(out=sp8[:], in_=p8b[:]),
                   waits=(("ct", ct),), incs=(("cv", 1),))
            p.emit("vector", lambda E: E.tensor_tensor(out=t8[:], in0=p8a[:], in1=sp8[:], op=ALU.add),
                   incs=(("cv", 1),))
            p.emit("vector", lambda E: E.tensor_scalar(out=mv2[:], in0=t8[:], scalar1=float(C0),
                                                       scalar2=None, op0=ALU.is_lt), incs=(("cv", 1),))
            p.emit("vector", lambda E: E.tensor_mul(out=mv2[:], in0=mv2[:], in1=mv[:]), incs=(("cv", 1),))
            p.emit("vector", lambda E: E.tensor_add(out=slot8[:], in0=t8[:], in1=eC0[:]), incs=(("cv", 1),))
            p.emit("vector", lambda E: E.tensor_copy(out=mv2b[:], in_=mv2[:]), incs=(("cv", 1),))
            cv = p.now("cv")
            for (ohx, wcol, rankoff) in ((oh1, 2, 0), (oh2, 3, TOK)):
                p.emit("vector", lambda E, ohx=ohx: E.tensor_mul(out=t8[:], in0=ohx[:], in1=slot8[:]),
                       waits=(("gs", p.now("gs")),), incs=(("cv", 1),))
                p.emit("vector", lambda E: E.tensor_reduce(out=sA[:], in_=t8[:], axis=AX.X, op=ALU.add),
                       incs=(("cv", 1),))
                p.emit("vector", lambda E, ohx=ohx: E.tensor_mul(out=t8[:], in0=ohx[:], in1=mv2[:]),
                       incs=(("cv", 1),))
                p.emit("vector", lambda E: E.tensor_reduce(out=vA[:], in_=t8[:], axis=AX.X, op=ALU.add),
                       incs=(("cv", 1),))
                p.emit("vector", lambda E: E.tensor_scalar(out=idxf[:], in0=vA[:], scalar1=-1e8,
                                                           scalar2=1e8, op0=ALU.mult, op1=ALU.add),
                       incs=(("cv", 1),))
                p.emit("vector", lambda E: E.tensor_add(out=idxf[:], in0=idxf[:], in1=sA[:]),
                       incs=(("cv", 1),))
                p.emit("vector", lambda E: E.tensor_copy(out=idxi[:], in_=idxf[:]), incs=(("cv", 1),))
                p.emit("vector", lambda E, blk=blk: E.tensor_copy(
                    out=svals[:, 0:1], in_=consts.lrow[:, blk:blk + 1]),
                    waits=(("gs", p.now("gs")),), incs=(("cv", 1),))
                p.emit("vector", lambda E, blk=blk, rankoff=rankoff: E.tensor_scalar(
                    out=idxf[:], in0=consts.lrow[:, blk:blk + 1], scalar1=float(rankoff),
                    scalar2=None, op0=ALU.add), incs=(("cv", 1),))
                p.emit("vector", lambda E: E.tensor_copy(out=svals[:, 1:2], in_=idxf[:]),
                       incs=(("cv", 1),))
                p.emit("vector", lambda E, wcol=wcol: E.tensor_copy(
                    out=svals[:, 2:3].bitcast(F32), in_=rt[:, wcol:wcol + 1]), incs=(("cv", 1),))
                cv = p.now("cv")
                p.emit("gpsimd", lambda E: E.indirect_dma_start(
                    out=rec_dram, out_offset=IndirectOffsetOnAxis(idxi[:], axis=0),
                    in_=svals[:], in_offset=None, bounds_check=bcreg["rec"], oob_is_err=False),
                    waits=(("cv", cv), ("st", p.now("st"))), incs=(("gs", 16),))
            p.emit("tensor", lambda E: E.matmul(p8b[0:1, :], consts.onescol[:], mv2b[:],
                                                start=True, stop=True),
                   waits=(("cv", cv),), incs=(("ct", 1),))
            ct = p.now("ct")
            p.emit("vector", lambda E: E.tensor_copy(out=cntrow[:], in_=p8b[0:1, :]),
                   waits=(("ct", ct),), incs=(("cv", 1),))
            p.emit("vector", lambda E: E.tensor_add(out=locoff[:], in0=locoff[:], in1=cntrow[:]),
                   incs=(("cv", 1),))
        gs_all = p.now("gs")

        for e in range(8):
            for gt in range(8):
                ld = p.dma("sync", recs[:], rec_dram[e * C0 + gt * 128: e * C0 + (gt + 1) * 128, :],
                           "ld", waits=(("gs", gs_all), ("cv", p.now("cv"))))
                p.emit("vector", lambda E, gt=gt: E.tensor_copy(out=idxi[:], in_=recs[:, 0:1]),
                       waits=(("ld", ld), ("gs", p.now("gs"))), incs=(("cv", 1),))
                p.emit("vector", lambda E, gt=gt: E.tensor_copy(out=adsti[:, gt:gt + 1], in_=recs[:, 1:2]),
                       incs=(("cv", 1),))
                p.emit("vector", lambda E, gt=gt: E.tensor_copy(out=wslot[:, gt:gt + 1],
                                                                in_=recs[:, 2:3].bitcast(F32)),
                       incs=(("cv", 1),))
                cv = p.now("cv")
                gl = p.emit("gpsimd", lambda E: E.indirect_dma_start(
                    out=xbt[:], out_offset=None, in_=xf_in,
                    in_offset=IndirectOffsetOnAxis(idxi[:], axis=0)),
                    waits=(("cv", cv), ("ct", p.now("ct"))), incs=(("gl", 16),), chain=False)
                emit_transpose_set(p, consts, lambda k: xbt[:, k * 128:(k + 1) * 128],
                                   None, 8, pT, wait=(("gl", gl),),
                                   dst_fn=lambda k, gt=gt: xgT[:, k * 1024 + gt * 128: k * 1024 + (gt + 1) * 128])
            for hc in range(32):
                wlv = p.dma("gpsimd", wch[:].rearrange("p (k m) -> p k m", k=8),
                            rg_in[e, :, hc * 128:(hc + 1) * 128].rearrange("(k p) m -> p k m", p=128),
                            "wl", waits=(("ct", p.now("ct")),))
                wlv = p.dma("gpsimd", wch2[:].rearrange("p (k m) -> p k m", k=8),
                            ru_in[e, :, hc * 128:(hc + 1) * 128].rearrange("(k p) m -> p k m", p=128),
                            "wl")
                tr_ok = p.after("cv")
                for sg2 in range(2):
                    for k in range(8):
                        p.emit("tensor", lambda E, k=k, sg2=sg2: E.matmul(
                            pH[0][:], wch[:, k * 128:(k + 1) * 128],
                            xgT[:, k * 1024 + sg2 * 512: k * 1024 + (sg2 + 1) * 512],
                            start=(k == 0), stop=(k == 7)),
                            waits=(("wl", wlv), ("ca", p.now("ca")), ("cv", p.now("cv"))) + tr_ok
                                  if k == 0 else (),
                            incs=((("ct", 1),) if k == 7 else ()))
                    for k in range(8):
                        p.emit("tensor", lambda E, k=k, sg2=sg2: E.matmul(
                            pH[1][:], wch2[:, k * 128:(k + 1) * 128],
                            xgT[:, k * 1024 + sg2 * 512: k * 1024 + (sg2 + 1) * 512],
                            start=(k == 0), stop=(k == 7)),
                            incs=((("ct", 1),) if k == 7 else ()))
                    ct = p.now("ct")
                    p.emit("scalar", lambda E: E.activation(out=sil[:], in_=pH[0][:], func=AF.Silu),
                           waits=(("ct", ct),), incs=(("ca", 1),))
                    ca = p.now("ca")
                    p.emit("vector", lambda E, hc=hc, sg2=sg2: E.tensor_tensor(
                        out=gbuf[:, hc * 1024 + sg2 * 512: hc * 1024 + (sg2 + 1) * 512],
                        in0=pH[1][:], in1=sil[:], op=ALU.mult),
                        waits=(("ca", ca),), incs=(("cv", 1),))
            g_done = p.after("cv")
            for nof in range(2):
                for wave in range(2):
                    for hc in range(32):
                        wlv = p.dma("gpsimd", wchd[:],
                                    rd_in[e, hc * 128:(hc + 1) * 128, nof * 512:(nof + 1) * 512],
                                    "wl", waits=(("ct", p.now("ct")),))
                        for sbi in range(4):
                            sb8 = wave * 4 + sbi
                            p.emit("tensor", lambda E, hc=hc, sb8=sb8, sbi=sbi: E.matmul(
                                pY[sbi][:], gbuf[:, hc * 1024 + sb8 * 128: hc * 1024 + (sb8 + 1) * 128],
                                wchd[:], start=(hc == 0), stop=(hc == 31)),
                                waits=((("wl", wlv),) + g_done + (("cv", p.now("cv")),)) if hc == 0
                                      else ((("wl", wlv),) if sbi == 0 else ()),
                                incs=(("ct", 1),))
                    ct = p.now("ct")
                    for sbi in range(4):
                        sb8 = wave * 4 + sbi
                        p.emit("vector", lambda E, sb8=sb8, sbi=sbi, nof=nof: E.tensor_scalar(
                            out=ybuf[:, sb8 * 1024 + nof * 512: sb8 * 1024 + (nof + 1) * 512],
                            in0=pY[sbi][:], scalar1=wslot[:, sb8:sb8 + 1], scalar2=None, op0=ALU.mult),
                            waits=(("ct", ct), ("gs", p.now("gs"))), incs=(("cv", 1),))
            cv = p.now("cv")
            for sb8 in range(8):
                p.emit("gpsimd", lambda E, sb8=sb8: E.indirect_dma_start(
                    out=a_dram, out_offset=IndirectOffsetOnAxis(adsti[:, sb8:sb8 + 1], axis=0),
                    in_=ybuf[:, sb8 * 1024:(sb8 + 1) * 1024], in_offset=None,
                    bounds_check=bcreg["a"], oob_is_err=False),
                    waits=(("cv", cv), ("st", p.now("st"))), incs=(("gs", 16),))
        gs_all2 = p.now("gs")

        for tg in range(8):
            for tb in range(4):
                blk = tg * 4 + tb
                ld = p.dma("sync", xbt[:], xf_in[blk * TPB:(blk + 1) * TPB, :], "ld",
                           waits=p.after("cv", "ct"))
                emit_transpose_set(p, consts, lambda k: xbt[:, k * 128:(k + 1) * 128],
                                   None, 8, pT, wait=(("ld", ld),),
                                   dst_fn=lambda k, tb=tb: xsT[:, k * 512 + tb * 128: k * 512 + (tb + 1) * 128])
            trd = p.after("cv")
            for hc in range(64):
                wlv = p.dma("gpsimd", wch[:].rearrange("p (k m) -> p k m", k=8),
                            sg_in[:, hc * 128:(hc + 1) * 128].rearrange("(k p) m -> p k m", p=128),
                            "wl", waits=(("ct", p.now("ct")),))
                wlv = p.dma("gpsimd", wch2[:].rearrange("p (k m) -> p k m", k=8),
                            su_in[:, hc * 128:(hc + 1) * 128].rearrange("(k p) m -> p k m", p=128),
                            "wl")
                for k in range(8):
                    p.emit("tensor", lambda E, k=k: E.matmul(
                        pH[0][:], wch[:, k * 128:(k + 1) * 128], xsT[:, k * 512:(k + 1) * 512],
                        start=(k == 0), stop=(k == 7)),
                        waits=(("wl", wlv), ("ca", p.now("ca")), ("cv", p.now("cv"))) + trd
                              if k == 0 else (),
                        incs=((("ct", 1),) if k == 7 else ()))
                for k in range(8):
                    p.emit("tensor", lambda E, k=k: E.matmul(
                        pH[1][:], wch2[:, k * 128:(k + 1) * 128], xsT[:, k * 512:(k + 1) * 512],
                        start=(k == 0), stop=(k == 7)),
                        incs=((("ct", 1),) if k == 7 else ()))
                ct = p.now("ct")
                p.emit("scalar", lambda E: E.activation(out=sil[:], in_=pH[0][:], func=AF.Silu),
                       waits=(("ct", ct),), incs=(("ca", 1),))
                ca = p.now("ca")
                p.emit("vector", lambda E, hc=hc: E.tensor_tensor(
                    out=gbuf[:, hc * 512:(hc + 1) * 512], in0=pH[1][:], in1=sil[:], op=ALU.mult),
                    waits=(("ca", ca),), incs=(("cv", 1),))
            g_done = p.after("cv")
            for nof in range(2):
                for hc in range(64):
                    wlv = p.dma("gpsimd", wchd[:],
                                sd_in[hc * 128:(hc + 1) * 128, nof * 512:(nof + 1) * 512],
                                "wl", waits=(("ct", p.now("ct")),))
                    for tb in range(4):
                        p.emit("tensor", lambda E, hc=hc, tb=tb: E.matmul(
                            pY[tb][:], gbuf[:, hc * 512 + tb * 128: hc * 512 + (tb + 1) * 128],
                            wchd[:], start=(hc == 0), stop=(hc == 63)),
                            waits=((("wl", wlv),) + g_done + (("cv", p.now("cv")),)) if hc == 0
                                  else ((("wl", wlv),) if tb == 0 else ()),
                            incs=(("ct", 1),))
                ct = p.now("ct")
                for tb in range(4):
                    p.emit("vector", lambda E, tb=tb, nof=nof: E.tensor_copy(
                        out=ybuf[:, tb * 1024 + nof * 512: tb * 1024 + (nof + 1) * 512],
                        in_=pY[tb][:]),
                        waits=(("ct", ct), ("gs", gs_all2)), incs=(("cv", 1),))
            shv = p.now("cv")
            for tb in range(4):
                blk = tg * 4 + tb
                ld = p.dma("sync", x1t[:], x1_in[blk * TPB:(blk + 1) * TPB, :], "ld",
                           waits=p.after("cv") + (("st", p.now("st")),))
                ld = p.dma("sync", a1t[:], a_dram[blk * TPB:(blk + 1) * TPB, :], "ld",
                           waits=(("gs", gs_all2),))
                ld = p.dma("sync", a2t[:], a_dram[TOK + blk * TPB: TOK + (blk + 1) * TPB, :], "ld")
                p.emit("vector", lambda E, tb=tb: E.tensor_add(
                    out=x1t[:], in0=x1t[:], in1=ybuf[:, tb * 1024:(tb + 1) * 1024]),
                    waits=(("ld", ld), ("cv", shv)), incs=(("cv", 1),))
                p.emit("vector", lambda E: E.tensor_add(out=x1t[:], in0=x1t[:], in1=a1t[:]),
                       incs=(("cv", 1),))
                p.emit("vector", lambda E: E.tensor_add(out=x1t[:], in0=x1t[:], in1=a2t[:]),
                       incs=(("cv", 1),))
                cv = p.now("cv")
                p.emit("sync", lambda E, blk=blk: E.dma_start(
                    out=x2_out[blk * TPB:(blk + 1) * TPB, :], in_=x1t[:]),
                    waits=(("cv", cv),), incs=(("st", 16),))
        p.emit("sync", lambda E: E.nop(), waits=(("st", p.now("st")),))
        finish_prog(nc, es, p)
    return nc


def run_C(inputs, resB):
    nc = build_C()
    counts_all = blocks_global_order([np.asarray(resB[c]["counts"]) for c in range(NCORE)])
    in_maps = []
    for c in range(NCORE):
        m = dict(host_consts(c))
        m["x1"] = np.asarray(resB[c]["x1"])
        m["xf"] = np.asarray(resB[c]["xf"])
        m["route"] = np.asarray(resB[c]["route"])
        m["counts_all"] = counts_all
        m["rg"] = inputs["L0_ffn_rg"]; m["ru"] = inputs["L0_ffn_ru"]; m["rd"] = inputs["L0_ffn_rd"]
        m["sg"] = inputs["L0_ffn_sg"]; m["su"] = inputs["L0_ffn_su"]; m["sd"] = inputs["L0_ffn_sd"]
        in_maps.append(m)
    res = run_bass_kernel_spmd(nc, in_maps, core_ids=list(range(NCORE)))
    return res.results




def build_D():
    nc = bass.Bass("TRN2", target_bir_lowering=False, debug=False)
    dr = decl_consts(nc)
    x2_in = nc.dram_tensor("x2", [TOK, D], F32, kind="ExternalInput").ap()
    wq_in = nc.dram_tensor("Wq", [D, D], F32, kind="ExternalInput").ap()
    wk_in = nc.dram_tensor("Wk", [D, D], F32, kind="ExternalInput").ap()
    wv_in = nc.dram_tensor("Wv", [D, D], F32, kind="ExternalInput").ap()
    bq_in = nc.dram_tensor("bq", [1, D], F32, kind="ExternalInput").ap()
    bk_in = nc.dram_tensor("bk", [1, D], F32, kind="ExternalInput").ap()
    bv_in = nc.dram_tensor("bv", [1, D], F32, kind="ExternalInput").ap()
    q_out = nc.dram_tensor("qbuf", [TOK, D], BF16, kind="ExternalOutput").ap()
    src_out = nc.dram_tensor("srcbuf", [TOK, D], BF16, kind="ExternalOutput").ap()
    kv_out = nc.dram_tensor("kvpart", [BATCH, 128, 1024], F32, kind="ExternalOutput").ap()
    ks_out = nc.dram_tensor("kspart", [BATCH, 64, 16], F32, kind="ExternalOutput").ap()

    es = ExitStack()
    sb = lambda name, shape, dt=F32: es.enter_context(nc.sbuf_tensor(name, shape, dt))
    ps = lambda name, shape, dt=F32: es.enter_context(nc.psum_tensor(name, shape, dt))
    with es:
        p = Prog(nc)
        consts = Consts(nc, es, p, dr)

        wq = sb("wq", [128, 8 * D], BF16)
        wk = sb("wk", [128, 8 * D], BF16)
        wv = sb("wv", [128, 8 * D], BF16)
        bqr = sb("bqr", [1, D], BF16); bkr = sb("bkr", [1, D], BF16); bvr = sb("bvr", [1, D], BF16)
        xb = sb("xb", [128, D])
        scr = sb("scr", [128, D])
        scr2 = sb("scr2", [128, D])
        mean = sb("mean", [128, 1]); rstd = sb("rstd", [128, 1]); tmp1 = sb("tmp1", [128, 1])
        srct = sb("srct", [128, D], BF16)
        srcT = sb("srcT", [128, 8 * 128], BF16)
        qt = sb("qt", [128, D])
        qb = sb("qb", [128, D], BF16)
        kb = sb("kb", [128, D], BF16)
        vb = sb("vb", [128, D], BF16)
        kvacc = sb("kvacc", [128, 8 * 128])
        ksacc = sb("ksacc", [64, 16])

        pT = [ps("pT0", [128, 128], BF16), ps("pT1", [128, 128], BF16)]
        pq = [ps("pq0", [128, 512]), ps("pq1", [128, 512])]
        pKV = [ps("pKV0", [128, 128]), ps("pKV1", [128, 128])]
        pKS = ps("pKS", [64, 16])

        for w_t, w_i in ((wq, wq_in), (wk, wk_in), (wv, wv_in)):
            p.dma("gpsimd", w_t[:].rearrange("p (k n) -> p k n", k=8),
                  w_i.rearrange("(k p) n -> p k n", p=128), "wl")
        p.dma("gpsimd", bqr[:], bq_in, "wl")
        p.dma("gpsimd", bkr[:], bk_in, "wl")
        p.dma("gpsimd", bvr[:], bv_in, "wl")
        wl0 = p.now("wl")

        for blk in range(NB):
            b, jj = blk // 4, blk % 4
            gate = p.after("blkf")
            ld = p.emit("sync", lambda E, blk=blk: E.dma_start(
                out=xb[:], in_=x2_in[blk * TPB:(blk + 1) * TPB, :]), waits=gate, incs=(("ld", 16),))
            st = emit_ln_stats(p, xb[:], scr[:], mean, rstd, tmp1, 1e-5, wait=(("ld", ld),))
            p.emit("vector", lambda E: E.tensor_scalar(out=srct[:], in0=xb[:], scalar1=mean[:],
                                                       scalar2=rstd[:], op0=ALU.subtract, op1=ALU.mult),
                   waits=(st,), incs=(("cv", 1),))
            cv = p.now("cv")
            p.emit("sync", lambda E, blk=blk: E.dma_start(
                out=src_out[blk * TPB:(blk + 1) * TPB, :], in_=srct[:]),
                waits=(("cv", cv),), incs=(("st", 16),))
            tr = emit_transpose_set(p, consts, lambda k: srct[:, k * 128:(k + 1) * 128], srcT, 8, pT,
                                    wait=p.after("cv"))
            for (w_t, b_t, outt) in ((wq, bqr, qt), (wk, bkr, kb), (wv, bvr, vb)):
                ev = p.now("cv")
                for n in range(2):
                    bank = n % 2
                    for k in range(8):
                        p.emit("tensor", lambda E, w_t=w_t, n=n, k=k, bank=bank: E.matmul(
                            pq[bank][:], srcT[:, k * 128:(k + 1) * 128],
                            w_t[:, k * D + n * 512: k * D + (n + 1) * 512],
                            start=(k == 0), stop=False),
                            waits=(tr, ("wl", wl0), ("cv", ev)) if k == 0 else (), incs=())
                    p.emit("tensor", lambda E, b_t=b_t, n=n, bank=bank: E.matmul(
                        pq[bank][:], consts.onesrow[0:1, 0:128], b_t[0:1, n * 512:(n + 1) * 512],
                        start=False, stop=True), incs=(("ct", 1),))
                    ct = p.now("ct")
                    if outt is qt or outt is vb:
                        p.emit("vector", lambda E, outt=outt, n=n, bank=bank: E.tensor_copy(
                            out=outt[:, n * 512:(n + 1) * 512], in_=pq[bank][:]),
                            waits=(("ct", ct),), incs=(("cv", 1),))
                    else:
                        p.emit("vector", lambda E, n=n, bank=bank: E.tensor_scalar_min(
                            out=scr[:, n * 512:(n + 1) * 512], in0=pq[bank][:], scalar1=0.0),
                            waits=(("ct", ct),), incs=(("cv", 1),))
                        p.emit("vector", lambda E, n=n, bank=bank: E.tensor_scalar_max(
                            out=scr2[:, n * 512:(n + 1) * 512], in0=pq[bank][:], scalar1=0.0),
                            incs=(("cv", 1),))
                        cv = p.now("cv")
                        p.emit("scalar", lambda E, n=n: E.activation(
                            out=scr[:, n * 512:(n + 1) * 512], in_=scr[:, n * 512:(n + 1) * 512],
                            func=AF.Exp), waits=(("cv", cv),), incs=(("ca", 1),))
                        ca = p.now("ca")
                        p.emit("vector", lambda E, outt=outt, n=n: E.tensor_add(
                            out=outt[:, n * 512:(n + 1) * 512], in0=scr[:, n * 512:(n + 1) * 512],
                            in1=scr2[:, n * 512:(n + 1) * 512]),
                            waits=(("ca", ca),), incs=(("cv", 1),))
            p.emit("vector", lambda E: E.tensor_scalar_min(out=scr[:], in0=qt[:], scalar1=0.0),
                   incs=(("cv", 1),))
            p.emit("vector", lambda E: E.tensor_scalar_max(out=scr2[:], in0=qt[:], scalar1=0.0),
                   incs=(("cv", 1),))
            cv = p.now("cv")
            p.emit("scalar", lambda E: E.activation(out=scr[:], in_=scr[:], func=AF.Exp),
                   waits=(("cv", cv),), incs=(("ca", 1),))
            ca = p.now("ca")
            p.emit("vector", lambda E: E.tensor_add(out=qb[:], in0=scr[:], in1=scr2[:]),
                   waits=(("ca", ca),), incs=(("cv", 1),))
            cv = p.now("cv")
            p.emit("sync", lambda E, blk=blk: E.dma_start(
                out=q_out[blk * TPB:(blk + 1) * TPB, :], in_=qb[:]),
                waits=(("cv", cv),), incs=(("st", 16),))
            kvcv = p.now("cv")
            for j in range(8):
                bank = j % 2
                p.emit("tensor", lambda E, j=j, bank=bank: E.matmul(
                    pKV[bank][:], kb[:, j * 128:(j + 1) * 128], vb[:, j * 128:(j + 1) * 128],
                    start=True, stop=True),
                    waits=(("cv", cv if j == 0 else kvcv),), incs=(("ct", 1),))
                ct = p.now("ct")
                if jj == 0:
                    p.emit("vector", lambda E, j=j, bank=bank: E.tensor_copy(
                        out=kvacc[:, j * 128:(j + 1) * 128], in_=pKV[bank][:]),
                        waits=(("ct", ct),), incs=(("cv", 1),))
                else:
                    p.emit("vector", lambda E, j=j, bank=bank: E.tensor_tensor(
                        out=kvacc[:, j * 128:(j + 1) * 128], in0=pKV[bank][:],
                        in1=kvacc[:, j * 128:(j + 1) * 128], op=ALU.add),
                        waits=(("ct", ct),), incs=(("cv", 1),))
                kvcv = p.now("cv")
            for h in range(16):
                p.emit("tensor", lambda E, h=h: E.matmul(
                    pKS[0:64, h:h + 1], kb[:, h * 64:(h + 1) * 64], consts.onescol[:],
                    start=True, stop=True),
                    waits=(("cv", p.now("cv")),) if h == 0 else (),
                    incs=((("ct", 1),) if h == 15 else ()))
            ct = p.now("ct")
            if jj == 0:
                p.emit("vector", lambda E: E.tensor_copy(out=ksacc[:], in_=pKS[0:64, :]),
                       waits=(("ct", ct),), incs=(("cv", 1),))
            else:
                p.emit("vector", lambda E: E.tensor_tensor(out=ksacc[:], in0=pKS[0:64, :],
                                                           in1=ksacc[:], op=ALU.add),
                       waits=(("ct", ct),), incs=(("cv", 1),))
            if jj == 3:
                cv = p.now("cv")
                p.emit("sync", lambda E, b=b: E.dma_start(out=kv_out[b], in_=kvacc[:]),
                       waits=(("cv", cv),), incs=(("st", 16),))
                p.emit("sync", lambda E, b=b: E.dma_start(out=ks_out[b], in_=ksacc[:]),
                       incs=(("st", 16),))
            st = p.now("st")
            p.emit("sync", lambda E: E.nop(), waits=(("st", st),), incs=(("blkf", 1),))
        p.emit("sync", lambda E: E.nop(), waits=(("st", p.now("st")),))
        finish_prog(nc, es, p)
    return nc


def run_D(inputs, resC):
    nc = build_D()
    in_maps = []
    for c in range(NCORE):
        m = dict(host_consts(c))
        m["x2"] = np.asarray(resC[c]["x2"])
        m["Wq"] = inputs["L1_attn_Wq"]; m["Wk"] = inputs["L1_attn_Wk"]; m["Wv"] = inputs["L1_attn_Wv"]
        m["bq"] = inputs["L1_attn_bq"].reshape(1, D)
        m["bk"] = inputs["L1_attn_bk"].reshape(1, D)
        m["bv"] = inputs["L1_attn_bv"].reshape(1, D)
        in_maps.append(m)
    res = run_bass_kernel_spmd(nc, in_maps, core_ids=list(range(NCORE)))
    return res.results




def build_E():
    nc = bass.Bass("TRN2", target_bir_lowering=False, debug=False)
    dr = decl_consts(nc)
    x2_in = nc.dram_tensor("x2", [TOK, D], F32, kind="ExternalInput").ap()
    q_in = nc.dram_tensor("qbuf", [TOK, D], BF16, kind="ExternalInput").ap()
    src_in = nc.dram_tensor("srcbuf", [TOK, D], BF16, kind="ExternalInput").ap()
    kv_in = nc.dram_tensor("kv_all", [NCORE, BATCH, 128, 1024], F32, kind="ExternalInput").ap()
    ks_in = nc.dram_tensor("ks_all", [NCORE, BATCH, 64, 16], F32, kind="ExternalInput").ap()
    wo_in = nc.dram_tensor("Wo", [D, D], F32, kind="ExternalInput").ap()
    bo_in = nc.dram_tensor("bo", [1, D], F32, kind="ExternalInput").ap()
    w1_in = nc.dram_tensor("W1", [D, 4 * D], F32, kind="ExternalInput").ap()
    b1_in = nc.dram_tensor("b1", [1, 4 * D], F32, kind="ExternalInput").ap()
    w2_in = nc.dram_tensor("W2", [4 * D, D], F32, kind="ExternalInput").ap()
    b2_in = nc.dram_tensor("b2", [1, D], F32, kind="ExternalInput").ap()
    g1_in = nc.dram_tensor("g1", [1, D], F32, kind="ExternalInput").ap()
    be1_in = nc.dram_tensor("be1", [1, D], F32, kind="ExternalInput").ap()
    g2_in = nc.dram_tensor("g2", [1, D], F32, kind="ExternalInput").ap()
    be2_in = nc.dram_tensor("be2", [1, D], F32, kind="ExternalInput").ap()
    x3_out = nc.dram_tensor("x3", [TOK, D], F32, kind="ExternalOutput").ap()
    dbg1 = nc.dram_tensor("dbg1", [128, D], F32, kind="ExternalOutput").ap()
    dbg2 = nc.dram_tensor("dbg2", [128, D], F32, kind="ExternalOutput").ap()
    dbg3 = nc.dram_tensor("dbg3", [128, 16], F32, kind="ExternalOutput").ap()
    dbg4 = nc.dram_tensor("dbg4", [128, D], F32, kind="ExternalOutput").ap()
    dbg5 = nc.dram_tensor("dbg5", [128, D], F32, kind="ExternalOutput").ap()
    dbg6 = nc.dram_tensor("dbg6", [128, D], F32, kind="ExternalOutput").ap()
    dbg7 = nc.dram_tensor("dbg7", [128, D], F32, kind="ExternalOutput").ap()
    dbg8 = nc.dram_tensor("dbg8", [128, 512], F32, kind="ExternalOutput").ap()
    dbg9 = nc.dram_tensor("dbg9", [128, D], F32, kind="ExternalOutput").ap()
    ks_dram = nc.dram_tensor("ksflat", [BATCH, NH * HD], F32).ap()

    es = ExitStack()
    sb = lambda name, shape, dt=F32: es.enter_context(nc.sbuf_tensor(name, shape, dt))
    ps = lambda name, shape, dt=F32: es.enter_context(nc.psum_tensor(name, shape, dt))
    with es:
        p = Prog(nc)
        consts = Consts(nc, es, p, dr)

        wo = sb("wo", [128, 8 * D], BF16)
        kvs = sb("kvs", [128, 1024])
        kvb = sb("kvb", [128, 8 * 1024], BF16)
        kss = sb("kss", [64, 8 * 16])
        tload = sb("tload", [128, 1024])
        bdiag = sb("bdiag", [128, 128], BF16)
        rowf = sb("rowf", [1, 4 * D])
        rowb = sb("rowb", [1, 4 * D], BF16)
        gb1 = sb("gb1", [128, D], BF16); bb1 = sb("bb1", [128, D], BF16)
        gb2 = sb("gb2", [128, D], BF16); bb2 = sb("bb2", [128, D], BF16)
        bor = sb("bor", [1, D], BF16)
        b1r = sb("b1r", [1, 4 * D], BF16)
        b2r = sb("b2r", [1, D], BF16)
        ksbb = sb("ksbb", [128, D], BF16)
        qt = sb("qt", [128, D], BF16)
        srct = sb("srct", [128, D], BF16)
        scr = sb("scr", [128, D])
        scr2 = sb("scr2", [128, D])
        z16 = sb("z16", [128, 16])
        qz = sb("qz", [128, D], BF16)
        qzT = sb("qzT", [128, 8 * 128], BF16)
        ofm = sb("ofm", [128, 8 * 128], BF16)
        x1a4 = sb("x1a4", [128, 4 * D])
        x24 = sb("x24", [128, 4 * D])
        xln = sb("xln", [128, D], BF16)
        xlT = sb("xlT", [128, 8 * 512], BF16)
        hfm = sb("hfm", [128, 32 * 512], BF16)
        wch = sb("wch", [128, 8 * 128], BF16)
        wchd = sb("wchd", [128, 512], BF16)
        mean = sb("mean", [128, 1]); rstd = sb("rstd", [128, 1]); tmp1 = sb("tmp1", [128, 1])
        x1bt = sb("x1bt", [128, D])

        pT = [ps("pT0", [128, 128], BF16), ps("pT1", [128, 128], BF16)]
        pO = [ps("pO0", [128, 128]), ps("pO1", [128, 128])]
        pq = [ps("pq0", [128, 512]), ps("pq1", [128, 512])]
        pY = [pq[0], pq[1], ps("pY2", [128, 512]), ps("pY3", [128, 512])]

        p.dma("gpsimd", bdiag[:], dr["c_bdiag"], "wl")
        wlb = p.now("wl")
        for b in range(BATCH):
            for r in range(NCORE):
                ld = p.dma("sync", tload[:], kv_in[r, b], "ld", waits=p.after("cv"))
                if r == 0:
                    p.emit("vector", lambda E: E.tensor_copy(out=kvs[:], in_=tload[:]),
                           waits=(("ld", ld),), incs=(("cv", 1),))
                else:
                    p.emit("vector", lambda E: E.tensor_add(out=kvs[:], in0=kvs[:], in1=tload[:]),
                           waits=(("ld", ld),), incs=(("cv", 1),))
            p.emit("vector", lambda E, b=b: E.tensor_tensor(
                out=kvb[:, b * 1024:(b + 1) * 1024].rearrange("p (j c) -> p j c", c=128),
                in0=kvs[:].rearrange("p (j c) -> p j c", c=128),
                in1=bdiag[:].unsqueeze(1).broadcast_to([128, 8, 128]),
                op=ALU.mult),
                waits=(("wl", wlb),), incs=(("cv", 1),))
        for b in range(BATCH):
            for r in range(NCORE):
                ld = p.dma("sync", tload[0:64, 0:16], ks_in[r, b], "ld", waits=p.after("cv"))
                if r == 0:
                    p.emit("vector", lambda E, b=b: E.tensor_copy(
                        out=kss[:, b * 16:(b + 1) * 16], in_=tload[0:64, 0:16]),
                        waits=(("ld", ld),), incs=(("cv", 1),))
                else:
                    p.emit("vector", lambda E, b=b: E.tensor_add(
                        out=kss[:, b * 16:(b + 1) * 16],
                        in0=kss[:, b * 16:(b + 1) * 16], in1=tload[0:64, 0:16]),
                        waits=(("ld", ld),), incs=(("cv", 1),))
        cv = p.now("cv")
        for b in range(BATCH):
            p.emit("sync", lambda E, b=b: E.dma_start(
                out=ks_dram[b].rearrange("(h d) -> d h", d=64),
                in_=kss[:, b * 16:(b + 1) * 16],
                allow_slow_non_contiguous=True),
                waits=(("cv", cv),), incs=(("st", 16),))
        st_ks = p.now("st")
        p.dma("gpsimd", wo[:].rearrange("p (k n) -> p k n", k=8),
              wo_in.rearrange("(k p) n -> p k n", p=128), "wl")
        p.dma("gpsimd", bor[:], bo_in, "wl")
        p.dma("gpsimd", b1r[:], b1_in, "wl")
        p.dma("gpsimd", b2r[:], b2_in, "wl")
        wl0 = p.now("wl")
        for (row_in, dstt) in ((g1_in, gb1), (be1_in, bb1), (g2_in, gb2), (be2_in, bb2)):
            ldw = p.dma("gpsimd", rowb[0:1, 0:D], row_in, "wl",
                        waits=(("ct", p.now("ct")),))
            for n2 in range(2):
                p.emit("tensor", lambda E, n2=n2: E.matmul(
                    pq[n2][:], consts.onesrow[0:1, 0:128], rowb[0:1, n2 * 512:(n2 + 1) * 512],
                    start=True, stop=True),
                    waits=(("wl", ldw), ("cv", p.now("cv"))) + consts.ready, incs=(("ct", 1),))
            ct = p.now("ct")
            for n2 in range(2):
                p.emit("vector", lambda E, dstt=dstt, n2=n2: E.tensor_copy(
                    out=dstt[:, n2 * 512:(n2 + 1) * 512], in_=pq[n2][:]),
                    waits=(("ct", ct),), incs=(("cv", 1),))
        gbready = p.after("cv")

        for tg in range(8):
            for tb in range(4):
                blk = tg * 4 + tb
                b = blk // 4
                gate = p.after("blkf") + p.after("cv")
                ld = p.dma("sync", qt[:], q_in[blk * TPB:(blk + 1) * TPB, :], "ld", waits=gate)
                ld = p.dma("sync", srct[:], src_in[blk * TPB:(blk + 1) * TPB, :], "ld")
                ld = p.dma("sync", x24[:, tb * D:(tb + 1) * D], x2_in[blk * TPB:(blk + 1) * TPB, :], "ld")
                if tb == 0:
                    ldk = p.dma("gpsimd", rowb[0:1, 0:D], ks_dram[b:b + 1, :], "wl",
                                waits=(("st", st_ks), ("ct", p.now("ct"))))
                    for n2 in range(2):
                        p.emit("tensor", lambda E, n2=n2: E.matmul(
                            pq[n2][:], consts.onesrow[0:1, 0:128], rowb[0:1, n2 * 512:(n2 + 1) * 512],
                            start=True, stop=True),
                            waits=(("wl", ldk), ("cv", p.now("cv"))), incs=(("ct", 1),))
                    ct = p.now("ct")
                    for n2 in range(2):
                        p.emit("vector", lambda E, n2=n2: E.tensor_copy(
                            out=ksbb[:, n2 * 512:(n2 + 1) * 512], in_=pq[n2][:]),
                            waits=(("ct", ct),), incs=(("cv", 1),))
                p.emit("vector", lambda E: E.tensor_tensor(out=scr[:], in0=qt[:], in1=ksbb[:],
                                                           op=ALU.mult),
                       waits=(("ld", ld),), incs=(("cv", 1),))
                p.emit("vector", lambda E: E.tensor_reduce(
                    out=z16[:].unsqueeze(2), in_=scr[:].rearrange("p (h c) -> p h c", c=64),
                    axis=AX.X, op=ALU.add), incs=(("cv", 1),))
                p.emit("vector", lambda E: E.tensor_scalar_add(out=z16[:], in0=z16[:], scalar1=1e-6),
                       incs=(("cv", 1),))
                p.emit("vector", lambda E: E.reciprocal(out=z16[:], in_=z16[:]), incs=(("cv", 1),))
                z3 = z16[:].unsqueeze(2).broadcast_to([128, 16, 64])
                p.emit("vector", lambda E: E.tensor_tensor(
                    out=qz[:].rearrange("p (h c) -> p h c", c=64),
                    in0=qt[:].rearrange("p (h c) -> p h c", c=64), in1=z3, op=ALU.mult),
                    incs=(("cv", 1),))
                tr = emit_transpose_set(p, consts, lambda k: qz[:, k * 128:(k + 1) * 128], qzT, 8, pT,
                                        wait=p.after("cv"))
                for j in range(8):
                    p.emit("tensor", lambda E, j=j, b=b: E.matmul(
                        pO[j % 2][:],
                        kvb[:, b * 1024 + j * 128: b * 1024 + (j + 1) * 128],
                        qzT[:, j * 128:(j + 1) * 128],
                        start=True, stop=True),
                        waits=(tr, ("cv", p.now("cv"))), incs=(("ct", 1),))
                    ct = p.now("ct")
                    p.emit("vector", lambda E, j=j: E.tensor_copy(
                        out=ofm[:, j * 128:(j + 1) * 128], in_=pO[j % 2][:]),
                        waits=(("ct", ct),), incs=(("cv", 1),))
                ofm_ready = p.after("cv")
                for n in range(2):
                    for k in range(8):
                        p.emit("tensor", lambda E, n=n, k=k: E.matmul(
                            pq[n][:], ofm[:, k * 128:(k + 1) * 128],
                            wo[:, k * D + n * 512: k * D + (n + 1) * 512],
                            start=(k == 0), stop=False),
                            waits=(("wl", wl0),) + ofm_ready + (("cv", p.now("cv")),) if k == 0 else (),
                            incs=())
                    p.emit("tensor", lambda E, n=n: E.matmul(
                        pq[n][:], consts.onesrow[0:1, 0:128], bor[0:1, n * 512:(n + 1) * 512],
                        start=False, stop=True), incs=(("ct", 1),))
                ct = p.now("ct")
                if blk == 0:
                    for n in range(2):
                        p.emit("vector", lambda E, n=n: E.tensor_copy(
                            out=scr2[:, n * 512:(n + 1) * 512], in_=pq[n][:]),
                            waits=(("ct", ct),), incs=(("cv", 1),))
                    cvd = p.now("cv")
                    p.emit("sync", lambda E: E.dma_start(out=dbg6, in_=scr2[:]),
                           waits=(("cv", cvd),), incs=(("st", 16),))
                    p.emit("gpsimd", lambda E: E.dma_start(out=dbg7, in_=srct[:]),
                           waits=(("cv", cvd),), incs=(("st", 16),))
                for n in range(2):
                    p.emit("vector", lambda E, n=n, tb=tb: E.tensor_tensor(
                        out=x1a4[:, tb * D + n * 512: tb * D + (n + 1) * 512],
                        in0=pq[n][:], in1=srct[:, n * 512:(n + 1) * 512], op=ALU.add),
                        waits=(("ct", ct),) + ((("st", p.now("st")),) if blk == 0 else ()),
                        incs=(("cv", 1),))
                if blk == 0:
                    cvd = p.now("cv")
                    p.emit("sync", lambda E: E.dma_start(out=dbg1, in_=x1a4[:, 0:D]),
                           waits=(("cv", cvd),), incs=(("st", 16),))
                    p.emit("gpsimd", lambda E: E.dma_start(out=dbg2, in_=ofm[:]),
                           waits=(("cv", cvd),), incs=(("st", 16),))
                    p.emit("sync", lambda E: E.dma_start(out=dbg3, in_=z16[:]),
                           waits=(("cv", cvd),), incs=(("st", 16),))
                    p.emit("gpsimd", lambda E: E.dma_start(out=dbg4, in_=ksbb[:]),
                           waits=(("cv", cvd),), incs=(("st", 16),))
                st = emit_ln_stats(p, x1a4[:, tb * D:(tb + 1) * D], scr[:], mean, rstd, tmp1, 1e-5,
                                   wait=p.after("cv") + ((("st", p.now("st")),) if blk == 0 else ()))
                p.emit("vector", lambda E, tb=tb: E.tensor_scalar(
                    out=scr[:], in0=x1a4[:, tb * D:(tb + 1) * D], scalar1=mean[:],
                    scalar2=rstd[:], op0=ALU.subtract, op1=ALU.mult),
                    waits=(st,), incs=(("cv", 1),))
                p.emit("vector", lambda E: E.tensor_mul(out=scr[:], in0=scr[:], in1=gb1[:]),
                       waits=gbready, incs=(("cv", 1),))
                p.emit("vector", lambda E: E.tensor_add(out=xln[:], in0=scr[:], in1=bb1[:]),
                       incs=(("cv", 1),))
                p.emit("vector", lambda E, tb=tb: E.tensor_copy(
                    out=x1a4[:, tb * D:(tb + 1) * D], in_=xln[:]), incs=(("cv", 1),))
                if blk == 0:
                    cvd = p.now("cv")
                    p.emit("gpsimd", lambda E: E.dma_start(out=dbg5, in_=xln[:]),
                           waits=(("cv", cvd),), incs=(("st", 16),))
                emit_transpose_set(p, consts, lambda k: xln[:, k * 128:(k + 1) * 128], None, 8, pT,
                                   wait=p.after("cv") + ((("st", p.now("st")),) if blk == 0 else ()),
                                   dst_fn=lambda k, tb=tb: xlT[:, k * 512 + tb * 128: k * 512 + (tb + 1) * 128])
            trd = p.after("cv")
            for hc in range(32):
                wlv = p.dma("gpsimd", wch[:].rearrange("p (k m) -> p k m", k=8),
                            w1_in[:, hc * 128:(hc + 1) * 128].rearrange("(k p) m -> p k m", p=128),
                            "wl", waits=(("ct", p.now("ct")),))
                for k in range(8):
                    p.emit("tensor", lambda E, hc=hc, k=k: E.matmul(
                        pY[0][:], wch[:, k * 128:(k + 1) * 128], xlT[:, k * 512:(k + 1) * 512],
                        start=(k == 0), stop=False),
                        waits=(("wl", wlv), ("ca", p.now("ca"))) + trd if k == 0 else (), incs=())
                p.emit("tensor", lambda E, hc=hc: E.matmul(
                    pY[0][:], b1r[0:1, hc * 128:(hc + 1) * 128], consts.onesrow[0:1, 0:512],
                    start=False, stop=True), incs=(("ct", 1),))
                ct = p.now("ct")
                p.emit("scalar", lambda E, hc=hc: E.activation(
                    out=hfm[:, hc * 512:(hc + 1) * 512], in_=pY[0][:], func=AF.Gelu),
                    waits=(("ct", ct),), incs=(("ca", 1),))
            if tg == 0:
                p.emit("gpsimd", lambda E: E.dma_start(out=dbg8, in_=hfm[:, 0:512]),
                       waits=p.after("ca"), incs=(("st", 16),))
            h_done = p.after("ca")
            for nof in range(2):
                for hc in range(32):
                    wlv = p.dma("gpsimd", wchd[:],
                                w2_in[hc * 128:(hc + 1) * 128, nof * 512:(nof + 1) * 512],
                                "wl", waits=(("ct", p.now("ct")),))
                    for tb in range(4):
                        p.emit("tensor", lambda E, hc=hc, tb=tb: E.matmul(
                            pY[tb][:], hfm[:, hc * 512 + tb * 128: hc * 512 + (tb + 1) * 128],
                            wchd[:], start=(hc == 0), stop=False),
                            waits=((("wl", wlv),) + h_done + (("cv", p.now("cv")),)) if hc == 0
                                  else ((("wl", wlv),) if tb == 0 else ()),
                            incs=(("ct", 1),))
                for tb in range(4):
                    p.emit("tensor", lambda E, tb=tb, nof=nof: E.matmul(
                        pY[tb][:], consts.onesrow[0:1, 0:128],
                        b2r[0:1, nof * 512:(nof + 1) * 512], start=False, stop=True),
                        incs=(("ct", 1),))
                ct = p.now("ct")
                for tb in range(4):
                    p.emit("vector", lambda E, tb=tb, nof=nof: E.tensor_tensor(
                        out=x1a4[:, tb * D + nof * 512: tb * D + (nof + 1) * 512],
                        in0=pY[tb][:], in1=x1a4[:, tb * D + nof * 512: tb * D + (nof + 1) * 512],
                        op=ALU.add),
                        waits=(("ct", ct),), incs=(("cv", 1),))
            if tg == 0:
                p.emit("sync", lambda E: E.dma_start(out=dbg9, in_=x1a4[:, 0:D]),
                       waits=p.after("cv"), incs=(("st", 16),))
            for tb in range(4):
                blk = tg * 4 + tb
                st = emit_ln_stats(p, x1a4[:, tb * D:(tb + 1) * D], scr[:], mean, rstd, tmp1, 1e-5,
                                   wait=p.after("cv") + ((("st", p.now("st")),) if tg == 0 else ()))
                p.emit("vector", lambda E, tb=tb: E.tensor_scalar(
                    out=scr[:], in0=x1a4[:, tb * D:(tb + 1) * D], scalar1=mean[:],
                    scalar2=rstd[:], op0=ALU.subtract, op1=ALU.mult),
                    waits=(st,), incs=(("cv", 1),))
                p.emit("vector", lambda E: E.tensor_mul(out=scr[:], in0=scr[:], in1=gb2[:]),
                       incs=(("cv", 1),))
                p.emit("vector", lambda E: E.tensor_add(out=scr[:], in0=scr[:], in1=bb2[:]),
                       incs=(("cv", 1),))
                p.emit("vector", lambda E, tb=tb: E.tensor_add(
                    out=x1bt[:], in0=scr[:], in1=x24[:, tb * D:(tb + 1) * D]),
                    incs=(("cv", 1),))
                cv = p.now("cv")
                p.emit("sync", lambda E, blk=blk: E.dma_start(
                    out=x3_out[blk * TPB:(blk + 1) * TPB, :], in_=x1bt[:]),
                    waits=(("cv", cv),), incs=(("st", 16),))
            st = p.now("st")
            p.emit("sync", lambda E: E.nop(), waits=(("st", st),), incs=(("blkf", 1),))
        p.emit("sync", lambda E: E.nop(), waits=(("st", p.now("st")),))
        finish_prog(nc, es, p)
    return nc


def run_E(inputs, resC, resD):
    nc = build_E()
    kv_all = np.stack([np.asarray(resD[c]["kvpart"]) for c in range(NCORE)])
    ks_all = np.stack([np.asarray(resD[c]["kspart"]) for c in range(NCORE)])
    in_maps = []
    for c in range(NCORE):
        m = dict(host_consts(c))
        m["x2"] = np.asarray(resC[c]["x2"])
        m["qbuf"] = np.asarray(resD[c]["qbuf"])
        m["srcbuf"] = np.asarray(resD[c]["srcbuf"])
        m["kv_all"] = kv_all
        m["ks_all"] = ks_all
        m["Wo"] = inputs["L1_attn_Wo"]; m["bo"] = inputs["L1_attn_bo"].reshape(1, D)
        m["W1"] = inputs["L1_attn_W1"]; m["b1"] = inputs["L1_attn_b1"].reshape(1, 4 * D)
        m["W2"] = inputs["L1_attn_W2"]; m["b2"] = inputs["L1_attn_b2"].reshape(1, D)
        m["g1"] = inputs["L1_attn_g1"].reshape(1, D); m["be1"] = inputs["L1_attn_be1"].reshape(1, D)
        m["g2"] = inputs["L1_attn_g2"].reshape(1, D); m["be2"] = inputs["L1_attn_be2"].reshape(1, D)
        in_maps.append(m)
    res = run_bass_kernel_spmd(nc, in_maps, core_ids=list(range(NCORE)))
    return res.results



C1 = 2048
SLOTS_F = 8 * C1
EH = 2816
EHC = 22
SH = 1536
SHC = 12


def build_F():
    nc = bass.Bass("TRN2", target_bir_lowering=False, debug=False)
    dr = decl_consts(nc)
    x3_in = nc.dram_tensor("x3", [TOK, D], F32, kind="ExternalInput").ap()
    gw_in = nc.dram_tensor("gate_w", [8, D], F32, kind="ExternalInput").ap()
    w1_in = nc.dram_tensor("w1", [8, D, EH], F32, kind="ExternalInput").ap()
    w2_in = nc.dram_tensor("w2", [8, EH, D], F32, kind="ExternalInput").ap()
    w3_in = nc.dram_tensor("w3", [8, D, EH], F32, kind="ExternalInput").ap()
    sw1_in = nc.dram_tensor("sw1", [D, SH], F32, kind="ExternalInput").ap()
    sw2_in = nc.dram_tensor("sw2", [SH, D], F32, kind="ExternalInput").ap()
    sw3_in = nc.dram_tensor("sw3", [D, SH], F32, kind="ExternalInput").ap()
    out_f = nc.dram_tensor("outx", [TOK, D], F32, kind="ExternalOutput").ap()

    rec_dram = nc.dram_tensor("rec", [SLOTS_F, 3], I32).ap()
    a_dram = nc.dram_tensor("abuf", [2 * TOK, D], BF16).ap()
    xt_dram = nc.dram_tensor("xtst", [TOK, D], BF16).ap()

    es = ExitStack()
    sb = lambda name, shape, dt=F32: es.enter_context(nc.sbuf_tensor(name, shape, dt))
    ps = lambda name, shape, dt=F32: es.enter_context(nc.psum_tensor(name, shape, dt))
    with es:
        p = Prog(nc)
        consts = Consts(nc, es, p, dr)

        gbuf = sb("gbuf", [128, EHC * 1024], BF16)
        xgT = sb("xgT", [128, 8 * 1024], BF16)
        ybuf = sb("ybuf", [128, 8 * 1024], BF16)
        wch = sb("wch", [128, 8 * 128], BF16)
        wch2 = sb("wch2", [128, 8 * 128], BF16)
        wchd = sb("wchd", [128, 512], BF16)
        sil = sb("sil", [128, 512])
        xb = sb("xb", [128, D])
        scr = sb("scr", [128, D])
        mean = sb("mean", [128, 1]); rstd = sb("rstd", [128, 1]); tmp1 = sb("tmp1", [128, 1])
        xt = sb("xtt", [128, D], BF16)
        xtT = sb("xtT", [128, 8 * 128], BF16)
        gwT = sb("gwT", [128, 64], BF16)
        gw8 = sb("gw8", [8, D], BF16)
        lg = sb("lg", [128, 8]); ex8 = sb("ex8", [128, 8]); pr8 = sb("pr8", [128, 8])
        eq8 = sb("eq8", [128, 8]); cd8 = sb("cd8", [128, 8]); oh1 = sb("oh1", [128, 8])
        oh2 = sb("oh2", [128, 8]); pr2 = sb("pr2", [128, 8]); m8 = sb("m8", [128, 8])
        mv2 = sb("mv2", [128, 8]); mv2b = sb("mv2b", [128, 8], BF16)
        m8b = sb("m8b", [128, 8], BF16)
        t8 = sb("t8", [128, 8]); slot8 = sb("slot8", [128, 8]); sp8 = sb("sp8", [128, 8])
        eC1 = sb("eC1", [128, 8])
        locoff = sb("locoff", [1, 8]); cntrow = sb("cntrow", [1, 8])
        mx1 = sb("mx1", [128, 1]); mx2 = sb("mx2", [128, 1]); i1 = sb("i1", [128, 1])
        i2 = sb("i2", [128, 1]); wsum = sb("wsum", [128, 1])
        sA = sb("sA", [128, 1]); vA = sb("vA", [128, 1]); idxf = sb("idxf", [128, 1])
        idxi = sb("idxi", [128, 1], I32)
        svals = sb("svals", [128, 3], I32)
        recs = sb("recs", [128, 3], I32)
        wslot = sb("wslot", [128, 8])
        adsti = sb("adsti", [128, 8], I32)
        xsT = sb("xsT", [128, 8 * 512], BF16)
        xbt = sb("xbt", [128, D], BF16)
        x3t = sb("x3t", [128, D])
        a1t = sb("a1t", [128, D], BF16); a2t = sb("a2t", [128, D], BF16)
        zt = sb("zt", [128, D], BF16)
        prefill = sb("prefill", [128, 128 * 3], I32)

        pT = [ps("pT0", [128, 128], BF16), ps("pT1", [128, 128], BF16)]
        pH = [ps("pH0", [128, 512]), ps("pH1", [128, 512])]
        pY2 = [ps("pY0", [128, 512]), ps("pY1", [128, 512])]
        pY = [pH[0], pH[1], pY2[0], pY2[1]]
        p8a = ps("p8a", [128, 8])
        p8b = ps("p8b", [128, 8])

        OOBROW = 1 << 20
        bcreg = {}
        def _init_regs(E):
            r1 = E.alloc_register("bc_rec"); bcreg["rec"] = r1
            r2 = E.alloc_register("bc_a"); bcreg["a"] = r2
            E.reg_mov(r1, SLOTS_F - 1)
            return E.reg_mov(r2, 2 * TOK - 1)
        p.emit("gpsimd", _init_regs)
        pv = prefill[:].rearrange("p (t c) -> p t c", c=3)
        p.emit("vector", lambda E: E.memset(pv[:, :, 0:1], 0), incs=(("cv", 1),))
        p.emit("vector", lambda E: E.memset(pv[:, :, 1:2], OOBROW), incs=(("cv", 1),))
        p.emit("vector", lambda E: E.memset(pv[:, :, 2:3], 0), incs=(("cv", 1),))
        p.emit("vector", lambda E: E.memset(zt[:], 0.0), incs=(("cv", 1),))
        p.emit("vector", lambda E: E.memset(locoff[:], 0.0), incs=(("cv", 1),))
        p.emit("vector", lambda E: E.tensor_scalar_mul(out=eC1[:], in0=consts.eidx[:], scalar1=float(C1)),
               waits=consts.ready, incs=(("cv", 1),))
        cv0 = p.now("cv")
        p.emit("sync", lambda E: E.dma_start(out=rec_dram.rearrange("(t p) c -> p t c", p=128),
                                             in_=pv),
               waits=(("cv", cv0),), incs=(("st", 16),))
        for z in range(2 * NB):
            p.emit("sync", lambda E, z=z: E.dma_start(out=a_dram[z * 128:(z + 1) * 128, :], in_=zt[:]),
                   waits=(("cv", cv0),), incs=(("st", 16),))
        p.dma("gpsimd", gw8[:], gw_in, "wl")
        wl0 = p.now("wl")
        for k in range(8):
            bank = k % 2
            p.emit("tensor", lambda E, k=k, bank=bank: E.transpose(
                pT[bank][:, 0:8], gw8[0:8, k * 128:(k + 1) * 128], consts.iden[0:8, 0:8]),
                waits=(("wl", wl0),) + consts.ready + (("cv", p.now("cv")),), incs=(("ct", 1),))
            ct = p.now("ct")
            p.emit("vector", lambda E, k=k, bank=bank: E.tensor_copy(
                out=gwT[:, k * 8:(k + 1) * 8], in_=pT[bank][:, 0:8]),
                waits=(("ct", ct),), incs=(("cv", 1),))
        gwT_ready = p.after("cv")

        for blk in range(NB):
            ld = p.dma("sync", xb[:], x3_in[blk * TPB:(blk + 1) * TPB, :], "ld",
                       waits=p.after("cv") + p.after("st"))
            st_ln = emit_ln_stats(p, xb[:], scr[:], mean, rstd, tmp1, 1e-5, wait=(("ld", ld),))
            p.emit("vector", lambda E: E.tensor_scalar(out=xt[:], in0=xb[:], scalar1=mean[:],
                                                       scalar2=rstd[:], op0=ALU.subtract, op1=ALU.mult),
                   waits=(st_ln,), incs=(("cv", 1),))
            cv = p.now("cv")
            p.emit("sync", lambda E, blk=blk: E.dma_start(
                out=xt_dram[blk * TPB:(blk + 1) * TPB, :], in_=xt[:]),
                waits=(("cv", cv),), incs=(("st", 16),))
            tr = emit_transpose_set(p, consts, lambda k: xt[:, k * 128:(k + 1) * 128], xtT, 8, pT,
                                    wait=p.after("cv"))
            for k in range(8):
                p.emit("tensor", lambda E, k=k: E.matmul(
                    p8a[:], xtT[:, k * 128:(k + 1) * 128], gwT[:, k * 8:(k + 1) * 8],
                    start=(k == 0), stop=(k == 7)),
                    waits=(tr,) + gwT_ready if k == 0 else (), incs=((("ct", 1),) if k == 7 else ()))
            ct = p.now("ct")
            p.emit("vector", lambda E: E.tensor_copy(out=lg[:], in_=p8a[:]),
                   waits=(("ct", ct),), incs=(("cv", 1),))
            p.emit("vector", lambda E: E.tensor_reduce(out=mx1[:], in_=lg[:], axis=AX.X, op=ALU.max),
                   incs=(("cv", 1),))
            p.emit("vector", lambda E: E.tensor_scalar(out=ex8[:], in0=lg[:], scalar1=mx1[:],
                                                       scalar2=None, op0=ALU.subtract), incs=(("cv", 1),))
            cv = p.now("cv")
            p.emit("scalar", lambda E: E.activation(out=pr8[:], in_=ex8[:], func=AF.Exp),
                   waits=(("cv", cv),), incs=(("ca", 1),))
            ca = p.now("ca")
            p.emit("vector", lambda E: E.tensor_reduce(out=wsum[:], in_=pr8[:], axis=AX.X, op=ALU.add),
                   waits=(("ca", ca),), incs=(("cv", 1),))
            p.emit("vector", lambda E: E.reciprocal(out=wsum[:], in_=wsum[:]), incs=(("cv", 1),))
            p.emit("vector", lambda E: E.tensor_scalar(out=pr8[:], in0=pr8[:], scalar1=wsum[:],
                                                       scalar2=None, op0=ALU.mult), incs=(("cv", 1),))
            p.emit("vector", lambda E: E.tensor_reduce(out=mx1[:], in_=pr8[:], axis=AX.X, op=ALU.max),
                   incs=(("cv", 1),))
            p.emit("vector", lambda E: E.tensor_scalar(out=eq8[:], in0=pr8[:], scalar1=mx1[:],
                                                       scalar2=None, op0=ALU.is_ge), incs=(("cv", 1),))
            p.emit("vector", lambda E: E.tensor_scalar(out=cd8[:], in0=eq8[:], scalar1=-1e9,
                                                       scalar2=1e9, op0=ALU.mult, op1=ALU.add),
                   incs=(("cv", 1),))
            p.emit("vector", lambda E: E.tensor_add(out=cd8[:], in0=cd8[:], in1=consts.eidx[:]),
                   incs=(("cv", 1),))
            p.emit("vector", lambda E: E.tensor_reduce(out=i1[:], in_=cd8[:], axis=AX.X, op=ALU.min),
                   incs=(("cv", 1),))
            p.emit("vector", lambda E: E.tensor_scalar(out=oh1[:], in0=consts.eidx[:], scalar1=i1[:],
                                                       scalar2=None, op0=ALU.is_equal), incs=(("cv", 1),))
            p.emit("vector", lambda E: E.tensor_scalar(out=pr2[:], in0=oh1[:], scalar1=-1.0,
                                                       scalar2=1.0, op0=ALU.mult, op1=ALU.add),
                   incs=(("cv", 1),))
            p.emit("vector", lambda E: E.tensor_mul(out=pr2[:], in0=pr2[:], in1=pr8[:]), incs=(("cv", 1),))
            p.emit("vector", lambda E: E.tensor_reduce(out=mx2[:], in_=pr2[:], axis=AX.X, op=ALU.max),
                   incs=(("cv", 1),))
            p.emit("vector", lambda E: E.tensor_scalar(out=eq8[:], in0=pr2[:], scalar1=mx2[:],
                                                       scalar2=None, op0=ALU.is_ge), incs=(("cv", 1),))
            p.emit("vector", lambda E: E.tensor_scalar(out=cd8[:], in0=eq8[:], scalar1=-1e9,
                                                       scalar2=1e9, op0=ALU.mult, op1=ALU.add),
                   incs=(("cv", 1),))
            p.emit("vector", lambda E: E.tensor_add(out=cd8[:], in0=cd8[:], in1=consts.eidx[:]),
                   incs=(("cv", 1),))
            p.emit("vector", lambda E: E.tensor_reduce(out=i2[:], in_=cd8[:], axis=AX.X, op=ALU.min),
                   incs=(("cv", 1),))
            p.emit("vector", lambda E: E.tensor_scalar(out=oh2[:], in0=consts.eidx[:], scalar1=i2[:],
                                                       scalar2=None, op0=ALU.is_equal), incs=(("cv", 1),))
            p.emit("vector", lambda E: E.tensor_add(out=m8[:], in0=oh1[:], in1=oh2[:]), incs=(("cv", 1),))
            p.emit("vector", lambda E: E.tensor_copy(out=m8b[:], in_=m8[:]), incs=(("cv", 1),))
            cv = p.now("cv")
            p.emit("tensor", lambda E: E.matmul(p8b[:], consts.utri[:], m8b[:], start=True, stop=True),
                   waits=(("cv", cv),), incs=(("ct", 1),))
            p.emit("tensor", lambda E: E.matmul(p8a[:], consts.onesrow_f[0:1, 0:128], locoff[:],
                                                start=True, stop=True), incs=(("ct", 1),))
            ct = p.now("ct")
            p.emit("vector", lambda E: E.tensor_copy(out=sp8[:], in_=p8b[:]),
                   waits=(("ct", ct),), incs=(("cv", 1),))
            p.emit("vector", lambda E: E.tensor_tensor(out=t8[:], in0=p8a[:], in1=sp8[:], op=ALU.add),
                   incs=(("cv", 1),))
            p.emit("vector", lambda E: E.tensor_scalar(out=mv2[:], in0=t8[:], scalar1=float(C1),
                                                       scalar2=None, op0=ALU.is_lt), incs=(("cv", 1),))
            p.emit("vector", lambda E: E.tensor_mul(out=mv2[:], in0=mv2[:], in1=m8[:]), incs=(("cv", 1),))
            p.emit("vector", lambda E: E.tensor_add(out=slot8[:], in0=t8[:], in1=eC1[:]), incs=(("cv", 1),))
            p.emit("vector", lambda E: E.tensor_copy(out=mv2b[:], in_=mv2[:]), incs=(("cv", 1),))
            cv = p.now("cv")
            for (ohx, wsrc, rankoff) in ((oh1, mx1, 0), (oh2, mx2, TOK)):
                p.emit("vector", lambda E, ohx=ohx: E.tensor_mul(out=t8[:], in0=ohx[:], in1=slot8[:]),
                       waits=(("gs", p.now("gs")),), incs=(("cv", 1),))
                p.emit("vector", lambda E: E.tensor_reduce(out=sA[:], in_=t8[:], axis=AX.X, op=ALU.add),
                       incs=(("cv", 1),))
                p.emit("vector", lambda E, ohx=ohx: E.tensor_mul(out=t8[:], in0=ohx[:], in1=mv2[:]),
                       incs=(("cv", 1),))
                p.emit("vector", lambda E: E.tensor_reduce(out=vA[:], in_=t8[:], axis=AX.X, op=ALU.add),
                       incs=(("cv", 1),))
                p.emit("vector", lambda E: E.tensor_scalar(out=idxf[:], in0=vA[:], scalar1=-1e8,
                                                           scalar2=1e8, op0=ALU.mult, op1=ALU.add),
                       incs=(("cv", 1),))
                p.emit("vector", lambda E: E.tensor_add(out=idxf[:], in0=idxf[:], in1=sA[:]),
                       incs=(("cv", 1),))
                p.emit("vector", lambda E: E.tensor_copy(out=idxi[:], in_=idxf[:]), incs=(("cv", 1),))
                p.emit("vector", lambda E, blk=blk: E.tensor_copy(
                    out=svals[:, 0:1], in_=consts.lrow[:, blk:blk + 1]), incs=(("cv", 1),))
                p.emit("vector", lambda E, blk=blk, rankoff=rankoff: E.tensor_scalar(
                    out=idxf[:], in0=consts.lrow[:, blk:blk + 1], scalar1=float(rankoff),
                    scalar2=None, op0=ALU.add), incs=(("cv", 1),))
                p.emit("vector", lambda E: E.tensor_copy(out=svals[:, 1:2], in_=idxf[:]),
                       incs=(("cv", 1),))
                p.emit("vector", lambda E, wsrc=wsrc: E.tensor_copy(
                    out=svals[:, 2:3].bitcast(F32), in_=wsrc[:]), incs=(("cv", 1),))
                cv = p.now("cv")
                p.emit("gpsimd", lambda E: E.indirect_dma_start(
                    out=rec_dram, out_offset=IndirectOffsetOnAxis(idxi[:], axis=0),
                    in_=svals[:], in_offset=None, bounds_check=bcreg["rec"], oob_is_err=False),
                    waits=(("cv", cv), ("st", p.now("st"))), incs=(("gs", 16),))
            p.emit("tensor", lambda E: E.matmul(p8b[0:1, :], consts.onescol[:], mv2b[:],
                                                start=True, stop=True),
                   waits=(("cv", cv),), incs=(("ct", 1),))
            ct = p.now("ct")
            p.emit("vector", lambda E: E.tensor_copy(out=cntrow[:], in_=p8b[0:1, :]),
                   waits=(("ct", ct),), incs=(("cv", 1),))
            p.emit("vector", lambda E: E.tensor_add(out=locoff[:], in0=locoff[:], in1=cntrow[:]),
                   incs=(("cv", 1),))
        gs_all = p.now("gs")

        for e in range(8):
            for half in range(2):
                base = e * C1 + half * 1024
                for gt in range(8):
                    ld = p.dma("sync", recs[:], rec_dram[base + gt * 128: base + (gt + 1) * 128, :],
                               "ld", waits=(("gs", gs_all), ("cv", p.now("cv"))))
                    p.emit("vector", lambda E: E.tensor_copy(out=idxi[:], in_=recs[:, 0:1]),
                           waits=(("ld", ld), ("gs", p.now("gs"))), incs=(("cv", 1),))
                    p.emit("vector", lambda E, gt=gt: E.tensor_copy(out=adsti[:, gt:gt + 1],
                                                                    in_=recs[:, 1:2]),
                           incs=(("cv", 1),))
                    p.emit("vector", lambda E, gt=gt: E.tensor_copy(out=wslot[:, gt:gt + 1],
                                                                    in_=recs[:, 2:3].bitcast(F32)),
                           incs=(("cv", 1),))
                    cv = p.now("cv")
                    gl = p.emit("gpsimd", lambda E: E.indirect_dma_start(
                        out=xbt[:], out_offset=None, in_=xt_dram,
                        in_offset=IndirectOffsetOnAxis(idxi[:], axis=0)),
                        waits=(("cv", cv), ("ct", p.now("ct")), ("st", p.now("st"))),
                        incs=(("gl", 16),), chain=False)
                    emit_transpose_set(p, consts, lambda k: xbt[:, k * 128:(k + 1) * 128],
                                       None, 8, pT, wait=(("gl", gl),),
                                       dst_fn=lambda k, gt=gt: xgT[:, k * 1024 + gt * 128: k * 1024 + (gt + 1) * 128])
                for hc in range(EHC):
                    wlv = p.dma("gpsimd", wch[:].rearrange("p (k m) -> p k m", k=8),
                                w1_in[e, :, hc * 128:(hc + 1) * 128].rearrange("(k p) m -> p k m", p=128),
                                "wl", waits=(("ct", p.now("ct")),))
                    wlv = p.dma("gpsimd", wch2[:].rearrange("p (k m) -> p k m", k=8),
                                w3_in[e, :, hc * 128:(hc + 1) * 128].rearrange("(k p) m -> p k m", p=128),
                                "wl")
                    tr_ok = p.after("cv")
                    for sg2 in range(2):
                        for k in range(8):
                            p.emit("tensor", lambda E, k=k, sg2=sg2: E.matmul(
                                pH[0][:], wch[:, k * 128:(k + 1) * 128],
                                xgT[:, k * 1024 + sg2 * 512: k * 1024 + (sg2 + 1) * 512],
                                start=(k == 0), stop=(k == 7)),
                                waits=(("wl", wlv), ("ca", p.now("ca")), ("cv", p.now("cv"))) + tr_ok
                                      if k == 0 else (),
                                incs=((("ct", 1),) if k == 7 else ()))
                        for k in range(8):
                            p.emit("tensor", lambda E, k=k, sg2=sg2: E.matmul(
                                pH[1][:], wch2[:, k * 128:(k + 1) * 128],
                                xgT[:, k * 1024 + sg2 * 512: k * 1024 + (sg2 + 1) * 512],
                                start=(k == 0), stop=(k == 7)),
                                incs=((("ct", 1),) if k == 7 else ()))
                        ct = p.now("ct")
                        p.emit("scalar", lambda E: E.activation(out=sil[:], in_=pH[0][:], func=AF.Silu),
                               waits=(("ct", ct),), incs=(("ca", 1),))
                        ca = p.now("ca")
                        p.emit("vector", lambda E, hc=hc, sg2=sg2: E.tensor_tensor(
                            out=gbuf[:, hc * 1024 + sg2 * 512: hc * 1024 + (sg2 + 1) * 512],
                            in0=pH[1][:], in1=sil[:], op=ALU.mult),
                            waits=(("ca", ca),), incs=(("cv", 1),))
                g_done = p.after("cv")
                for nof in range(2):
                    for wave in range(2):
                        for hc in range(EHC):
                            wlv = p.dma("gpsimd", wchd[:],
                                        w2_in[e, hc * 128:(hc + 1) * 128, nof * 512:(nof + 1) * 512],
                                        "wl", waits=(("ct", p.now("ct")),))
                            for sbi in range(4):
                                sb8 = wave * 4 + sbi
                                p.emit("tensor", lambda E, hc=hc, sb8=sb8, sbi=sbi: E.matmul(
                                    pY[sbi][:], gbuf[:, hc * 1024 + sb8 * 128: hc * 1024 + (sb8 + 1) * 128],
                                    wchd[:], start=(hc == 0), stop=(hc == EHC - 1)),
                                    waits=((("wl", wlv),) + g_done + (("cv", p.now("cv")),)) if hc == 0
                                          else ((("wl", wlv),) if sbi == 0 else ()),
                                    incs=(("ct", 1),))
                        ct = p.now("ct")
                        for sbi in range(4):
                            sb8 = wave * 4 + sbi
                            p.emit("vector", lambda E, sb8=sb8, sbi=sbi, nof=nof: E.tensor_scalar(
                                out=ybuf[:, sb8 * 1024 + nof * 512: sb8 * 1024 + (nof + 1) * 512],
                                in0=pY[sbi][:], scalar1=wslot[:, sb8:sb8 + 1], scalar2=None,
                                op0=ALU.mult),
                                waits=(("ct", ct), ("gs", p.now("gs"))), incs=(("cv", 1),))
                cv = p.now("cv")
                for sb8 in range(8):
                    p.emit("gpsimd", lambda E, sb8=sb8: E.indirect_dma_start(
                        out=a_dram, out_offset=IndirectOffsetOnAxis(adsti[:, sb8:sb8 + 1], axis=0),
                        in_=ybuf[:, sb8 * 1024:(sb8 + 1) * 1024], in_offset=None,
                        bounds_check=bcreg["a"], oob_is_err=False),
                        waits=(("cv", cv), ("st", p.now("st"))), incs=(("gs", 16),))
        gs_all2 = p.now("gs")

        for tg in range(8):
            for tb in range(4):
                blk = tg * 4 + tb
                ld = p.dma("sync", xbt[:], xt_dram[blk * TPB:(blk + 1) * TPB, :], "ld",
                           waits=p.after("cv", "ct") + (("gl", p.now("gl")),))
                emit_transpose_set(p, consts, lambda k: xbt[:, k * 128:(k + 1) * 128],
                                   None, 8, pT, wait=(("ld", ld),),
                                   dst_fn=lambda k, tb=tb: xsT[:, k * 512 + tb * 128: k * 512 + (tb + 1) * 128])
            trd = p.after("cv")
            for hc in range(SHC):
                wlv = p.dma("gpsimd", wch[:].rearrange("p (k m) -> p k m", k=8),
                            sw1_in[:, hc * 128:(hc + 1) * 128].rearrange("(k p) m -> p k m", p=128),
                            "wl", waits=(("ct", p.now("ct")),))
                wlv = p.dma("gpsimd", wch2[:].rearrange("p (k m) -> p k m", k=8),
                            sw3_in[:, hc * 128:(hc + 1) * 128].rearrange("(k p) m -> p k m", p=128),
                            "wl")
                for k in range(8):
                    p.emit("tensor", lambda E, k=k: E.matmul(
                        pH[0][:], wch[:, k * 128:(k + 1) * 128], xsT[:, k * 512:(k + 1) * 512],
                        start=(k == 0), stop=(k == 7)),
                        waits=(("wl", wlv), ("ca", p.now("ca")), ("cv", p.now("cv"))) + trd
                              if k == 0 else (),
                        incs=((("ct", 1),) if k == 7 else ()))
                for k in range(8):
                    p.emit("tensor", lambda E, k=k: E.matmul(
                        pH[1][:], wch2[:, k * 128:(k + 1) * 128], xsT[:, k * 512:(k + 1) * 512],
                        start=(k == 0), stop=(k == 7)),
                        incs=((("ct", 1),) if k == 7 else ()))
                ct = p.now("ct")
                p.emit("scalar", lambda E: E.activation(out=sil[:], in_=pH[0][:], func=AF.Silu),
                       waits=(("ct", ct),), incs=(("ca", 1),))
                ca = p.now("ca")
                p.emit("vector", lambda E, hc=hc: E.tensor_tensor(
                    out=gbuf[:, hc * 512:(hc + 1) * 512], in0=pH[1][:], in1=sil[:], op=ALU.mult),
                    waits=(("ca", ca),), incs=(("cv", 1),))
            g_done = p.after("cv")
            for nof in range(2):
                for hc in range(SHC):
                    wlv = p.dma("gpsimd", wchd[:],
                                sw2_in[hc * 128:(hc + 1) * 128, nof * 512:(nof + 1) * 512],
                                "wl", waits=(("ct", p.now("ct")),))
                    for tb in range(4):
                        p.emit("tensor", lambda E, hc=hc, tb=tb: E.matmul(
                            pY[tb][:], gbuf[:, hc * 512 + tb * 128: hc * 512 + (tb + 1) * 128],
                            wchd[:], start=(hc == 0), stop=(hc == SHC - 1)),
                            waits=((("wl", wlv),) + g_done + (("cv", p.now("cv")),)) if hc == 0
                                  else ((("wl", wlv),) if tb == 0 else ()),
                            incs=(("ct", 1),))
                ct = p.now("ct")
                for tb in range(4):
                    p.emit("vector", lambda E, tb=tb, nof=nof: E.tensor_copy(
                        out=ybuf[:, tb * 1024 + nof * 512: tb * 1024 + (nof + 1) * 512],
                        in_=pY[tb][:]),
                        waits=(("ct", ct), ("gs", gs_all2)), incs=(("cv", 1),))
            shv = p.now("cv")
            for tb in range(4):
                blk = tg * 4 + tb
                ld = p.dma("sync", x3t[:], x3_in[blk * TPB:(blk + 1) * TPB, :], "ld",
                           waits=p.after("cv") + (("st", p.now("st")),))
                ld = p.dma("sync", a1t[:], a_dram[blk * TPB:(blk + 1) * TPB, :], "ld",
                           waits=(("gs", gs_all2),))
                ld = p.dma("sync", a2t[:], a_dram[TOK + blk * TPB: TOK + (blk + 1) * TPB, :], "ld")
                p.emit("vector", lambda E, tb=tb: E.tensor_add(
                    out=x3t[:], in0=x3t[:], in1=ybuf[:, tb * 1024:(tb + 1) * 1024]),
                    waits=(("ld", ld), ("cv", shv)), incs=(("cv", 1),))
                p.emit("vector", lambda E: E.tensor_add(out=x3t[:], in0=x3t[:], in1=a1t[:]),
                       incs=(("cv", 1),))
                p.emit("vector", lambda E: E.tensor_add(out=x3t[:], in0=x3t[:], in1=a2t[:]),
                       incs=(("cv", 1),))
                cv = p.now("cv")
                p.emit("sync", lambda E, blk=blk: E.dma_start(
                    out=out_f[blk * TPB:(blk + 1) * TPB, :], in_=x3t[:]),
                    waits=(("cv", cv),), incs=(("st", 16),))
        p.emit("sync", lambda E: E.nop(), waits=(("st", p.now("st")),))
        finish_prog(nc, es, p)
    return nc


def run_F(inputs, resE):
    nc = build_F()
    in_maps = []
    for c in range(NCORE):
        m = dict(host_consts(c))
        m["x3"] = np.asarray(resE[c]["x3"])
        m["gate_w"] = inputs["L1_ffn_gate_w"]
        m["w1"] = inputs["L1_ffn_w1"]; m["w2"] = inputs["L1_ffn_w2"]; m["w3"] = inputs["L1_ffn_w3"]
        m["sw1"] = inputs["L1_ffn_sw1"]; m["sw2"] = inputs["L1_ffn_sw2"]; m["sw3"] = inputs["L1_ffn_sw3"]
        in_maps.append(m)
    res = run_bass_kernel_spmd(nc, in_maps, core_ids=list(range(NCORE)))
    return res.results




def kernel(**inputs):
    x = np.ascontiguousarray(np.asarray(inputs["L0_attn_x"], dtype=np.float32))
    xs_parts = core_slice_tokens(x)
    resA = run_A(inputs, xs_parts)
    resB = run_B(inputs, xs_parts, resA)
    resC = run_C(inputs, resB)
    resD = run_D(inputs, resC)
    resE = run_E(inputs, resC, resD)
    resF = run_F(inputs, resE)
    out = core_unslice_tokens([np.asarray(resF[c]["outx"], np.float32) for c in range(NCORE)])
    return out.astype(np.float32)
